# version 02
# baseline (speedup 1.0000x reference)
import numpy as np
import ml_dtypes

import concourse.bass as bass
import concourse.mybir as mybir
from concourse.bass_utils import run_bass_kernel_spmd

BF16 = mybir.dt.bfloat16
F32 = mybir.dt.float32
U8 = mybir.dt.uint8
AF = mybir.ActivationFunctionType
ALU = mybir.AluOpType

B, S, D = 2, 4096, 2048
T = 1024
H = 2
HID = 5632
DC = D // 128
HC = HID // 128
TT = T // 512
SHIFT = 6.0
SCALE = 1.0 / 32.0
EPS = 1e-5


class Prog:
    ENGINES = ("sync", "gpsimd", "tensor", "vector", "scalar")

    def __init__(self, nc):
        self.nc = nc
        self.ops = {e: [] for e in self.ENGINES}
        self.sems = {}
        self.cnt = {}

    def sem(self, name):
        h = self.nc.semaphore(name).__enter__()
        self.sems[name] = h
        self.cnt[name] = 0

    def inc(self, name, amt=1):
        self.cnt[name] += amt
        return self.cnt[name]

    def cur(self, name):
        return self.cnt[name]

    def emit(self, eng, fn):
        self.ops[eng].append(fn)

    def wait(self, eng, name, val):
        if val <= 0:
            return
        h = self.sems[name]
        self.emit(eng, lambda e, h=h, v=val: e.wait_ge(h, v))


def build_nc(taps=()):
    nc = bass.Bass("TRN2", target_bir_lowering=False, debug=False)

    x0_in = nc.dram_tensor("x0fm", [D, T], BF16, kind="ExternalInput").ap()
    wqT = nc.dram_tensor("wqT", [D, D], BF16, kind="ExternalInput").ap()
    wkT = nc.dram_tensor("wkT", [D, D], BF16, kind="ExternalInput").ap()
    wvT = nc.dram_tensor("wvT", [D, D], BF16, kind="ExternalInput").ap()
    woT = nc.dram_tensor("woT", [D, D], BF16, kind="ExternalInput").ap()
    bq_in = nc.dram_tensor("bq", [128, DC], F32, kind="ExternalInput").ap()
    bk_in = nc.dram_tensor("bk", [128, DC], F32, kind="ExternalInput").ap()
    bo_in = nc.dram_tensor("bo", [128, DC], F32, kind="ExternalInput").ap()
    bv_in = nc.dram_tensor("bv", [1, D], BF16, kind="ExternalInput").ap()
    wgT = nc.dram_tensor("wgT", [D, HID], U8, kind="ExternalInput").ap()
    wuT = nc.dram_tensor("wuT", [D, HID], U8, kind="ExternalInput").ap()
    wdT = nc.dram_tensor("wdT", [HID, D], U8, kind="ExternalInput").ap()
    sg_in = nc.dram_tensor("sg", [128, HC], F32, kind="ExternalInput").ap()
    su_in = nc.dram_tensor("su", [128, HC], F32, kind="ExternalInput").ap()
    sd_in = nc.dram_tensor("sd", [128, DC], F32, kind="ExternalInput").ap()
    id_in = nc.dram_tensor("idf", [128, 128], F32, kind="ExternalInput").ap()

    out = nc.dram_tensor("out", [T, D], F32, kind="ExternalOutput").ap()
    tap_t = {}
    for tn, shape, dt in (("t_k", [128, DC * T], BF16), ("t_v", [128, 8 * D], BF16),
                          ("t_q", [128, DC * T], BF16), ("t_y", [128, DC * T], BF16),
                          ("t_z", [128, DC * T], BF16), ("t_x1", [128, DC * T], BF16),
                          ("t_h", [128, HC * 512], BF16), ("t_dn", [128, DC * 512], F32),
                          ("t_x2", [128, DC * 512], F32),
                          ("t_p", [128, 32 * 512], BF16)):
        if tn in taps:
            tap_t[tn] = nc.dram_tensor(tn, shape, dt, kind="ExternalOutput").ap()

    cc_in_k = nc.dram_tensor("cc_in_k", [D, T // 2], F32, kind="Internal").ap()
    cc_out_k = [nc.dram_tensor(f"cc_out_k{h}", [4 * D // 2, T // 2], F32,
                               kind="Internal").ap() for h in range(2)]
    cc_in_v = [nc.dram_tensor(f"cc_in_v{h}", [T, D // 4], F32,
                              kind="Internal").ap() for h in range(2)]
    cc_out_v = [nc.dram_tensor(f"cc_out_v{h}", [4 * T, D // 4], F32,
                               kind="Internal").ap() for h in range(2)]
    RG = [[0, 1, 2, 3], [4, 5, 6, 7]]

    AVAIL = nc.sbuf_bytes_remaining
    NB = AVAIL // 256 * 256
    arena = nc.alloc_sbuf_tensor("arena", [128, NB], U8)

    def carve(off, shape, dtype, nparts=128):
        es = mybir.dt.size(dtype)
        n = es
        for s in shape[1:]:
            n *= s
        assert off % 64 == 0 and off + n <= NB, (off, n, NB)
        ap = arena[0:nparts, off:off + n].bitcast(dtype)
        if len(shape) == 2:
            return ap
        if len(shape) == 3:
            return ap.rearrange("p (a b) -> p a b", a=shape[1])
        raise AssertionError(shape)

    KB = 1024
    x0 = carve(0, [128, DC, T], BF16)
    harena = carve(0, [128, HC, 512], BF16)
    zarena = carve(32 * KB, [128, DC, T], BF16)
    qf = zarena
    otile = carve(44 * KB, [128, D], F32)
    csm = carve(52 * KB, [1, T], F32, 1)
    hsm = carve(56 * KB, [1, 512], F32, 1)
    x1 = carve(64 * KB, [128, DC, T], BF16)
    kown = x1
    vown = carve(96 * KB, [128, 8, D], BF16)
    pview = carve(96 * KB, [128, 32, 512], BF16)
    zsq = carve(96 * KB, [128, DC, 512], BF16)
    f32a = carve(112 * KB, [128, 512], F32)
    f32b = carve(114 * KB, [128, 512], F32)
    atile = [carve(116 * KB, [128, 512], BF16), carve(117 * KB, [128, 512], BF16)]
    stat = carve(118 * KB, [1, 5 * 512], F32, 1)
    idf = carve(58 * KB, [128, 128], F32)
    yf = carve(128 * KB, [128, DC, T], BF16)
    z2 = carve(128 * KB, [128, DC, 512], F32)
    bvb = carve(128 * KB, [1, D], BF16, 1)
    wb = [carve(160 * KB, [128, DC, 512], BF16),
          carve(176 * KB, [128, DC, 512], BF16)]
    wbd = [carve(160 * KB, [128, HC, 128], BF16),
           carve(176 * KB, [128, HC, 128], BF16)]
    kb = [carve(192 * KB, [128, 8, 128], BF16), carve(194 * KB, [128, 8, 128], BF16)]
    vb = [carve(196 * KB, [128, 8, 128], BF16), carve(198 * KB, [128, 8, 128], BF16)]
    recd = carve(200 * KB, [1, 512], F32, 1)
    rrep = carve(202 * KB, [128, 512], F32)
    rrep2 = carve(204 * KB, [128, 512], F32)
    sm = 206 * KB
    bqf = carve(sm, [128, DC], F32); sm += 64
    bkf = carve(sm, [128, DC], F32); sm += 64
    bof = carve(sm, [128, DC], F32); sm += 64
    sdf = carve(sm, [128, DC], F32); sm += 64
    sgf = carve(sm, [128, HC], F32); sm += 192
    suf = carve(sm, [128, HC], F32); sm += 192
    ones_c = carve(sm, [128, 1], BF16); sm += 64
    ones_cf = carve(sm, [128, 1], F32); sm += 64
    ones_rb = carve(sm, [1, 128], BF16, 1); sm += 256
    ones_r = carve(sm, [1, 128], F32, 1); sm += 512
    nshift = carve(sm, [128, 1], F32); sm += 64
    assert sm <= NB

    P = [nc.alloc_psum_tensor(f"P{i}", [128, 512], F32) for i in range(6)]
    PS1 = nc.alloc_psum_tensor("PS1", [1, 512], F32)
    PS2 = nc.alloc_psum_tensor("PS2", [1, 512], F32)

    p = Prog(nc)
    for s in ("s_in", "s_ccin", "s_ag", "s_mm", "s_sc", "s_ve", "s_out",
              "s_wl0", "s_wl1", "s_wu0", "s_wu1",
              "s_kbl0", "s_kbl1", "s_vbl0", "s_vbl1", "s_kbu", "s_vbu"):
        p.sem(s)

    def tap(tn, src_ap, sem_name):
        if tn not in tap_t:
            return
        p.wait("sync", sem_name, p.cur(sem_name))
        p.emit("sync", lambda s, tn=tn, a=src_ap: s.dma_start(
            out=tap_t[tn], in_=a).then_inc(p.sems["s_out"], 16))
        p.inc("s_out", 16)

    bank_free = {i: ("s_sc", 0) for i in list(range(6)) + ["PS1", "PS2"]}

    def wait_bank(bank, eng="tensor"):
        sname, val = bank_free[bank]
        p.wait(eng, sname, val)

    wl_cnt = [0, 0]
    wu_mm = [0, 0]

    def load_wslice(src_ap, ws, par, u8=False, down=False):
        eng = "gpsimd" if u8 else "sync"
        p.wait(eng, "s_mm", wu_mm[par])
        if down:
            dst = wbd[par][:]
            src = src_ap.rearrange("(c p) o -> p c o", p=128)[:, :, ws * 128:(ws + 1) * 128]
        else:
            dst = wb[par][:]
            src = src_ap.rearrange("(c p) o -> p c o", p=128)[:, :, ws * 512:(ws + 1) * 512]
        p.emit(eng, lambda e, dst=dst, src=src, par=par: e.dma_start(
            out=dst, in_=src).then_inc(p.sems[f"s_wl{par}"], 16))
        p.inc(f"s_wl{par}", 16)

    def use_wslice(par):
        p.wait("tensor", f"s_wl{par}", p.cur(f"s_wl{par}"))

    def done_wslice(par, last_mm_inst=None):
        wu_mm[par] = p.cur("s_mm")

    def mm_group(bank, psum_ap, mms, evac_eng, evac_fn):
        wait_bank(bank)
        n = len(mms)
        for j, emitfn in enumerate(mms):
            inc = (j == n - 1)
            p.emit("tensor", lambda t, f=emitfn, j=j, inc=inc:
                   (f(t, j == 0, j == n - 1).then_inc(p.sems["s_mm"], 1)
                    if inc else f(t, j == 0, j == n - 1)))
        mmv = p.inc("s_mm", 1)
        p.wait(evac_eng, "s_mm", mmv)
        sem = "s_sc" if evac_eng == "scalar" else "s_ve"
        p.emit(evac_eng, lambda e, f=evac_fn, sem=sem: f(e).then_inc(p.sems[sem], 1))
        ev = p.inc(sem, 1)
        bank_free[bank] = (sem, ev)
        return mmv, ev

    loads_sync = [(bvb, bv_in), (sgf, sg_in), (suf, su_in), (sdf, sd_in),
                  (bqf, bq_in), (bkf, bk_in), (bof, bo_in)]
    for t, src in loads_sync:
        p.emit("sync", lambda s, t=t, src=src: s.dma_start(
            out=t, in_=src).then_inc(p.sems["s_in"], 16))
        p.inc("s_in", 16)
    p.emit("sync", lambda s: s.dma_start(
        out=x0, in_=x0_in.rearrange("(c p) t -> p c t", p=128)
    ).then_inc(p.sems["s_in"], 16))
    p.inc("s_in", 16)
    SETUP_IN = p.cur("s_in")
    p.emit("vector", lambda v: v.memset(ones_c, 1.0))
    p.emit("vector", lambda v: v.memset(ones_cf, 1.0))
    p.emit("vector", lambda v: v.memset(ones_rb, 1.0))
    p.emit("vector", lambda v: v.memset(ones_r, 1.0))
    p.emit("vector", lambda v: v.memset(nshift, -SHIFT).then_inc(p.sems["s_ve"], 1))
    CONST_VE = p.inc("s_ve", 1)

    p.wait("tensor", "s_in", SETUP_IN)
    p.wait("tensor", "s_ve", CONST_VE)
    p.wait("scalar", "s_in", SETUP_IN)

    rot = [0]

    def proj_wstat(wsrc, bias, dst):
        for ws in range(4):
            par = ws % 2
            load_wslice(wsrc, ws, par)
            use_wslice(par)
            for tt in range(TT):
                for oc2 in range(4):
                    oc = ws * 4 + oc2
                    bank = rot[0] % 2
                    rot[0] += 1
                    mms = [
                        (lambda t, st, sp, par=par, i=i, oc2=oc2, tt=tt, bank=bank:
                         t.matmul(P[bank][:],
                                  wb[par][:, i, oc2 * 128:(oc2 + 1) * 128],
                                  qf[:, i, tt * 512:(tt + 1) * 512] if False else
                                  x0[:, i, tt * 512:(tt + 1) * 512],
                                  start=st, stop=sp))
                        for i in range(DC)]
                    def evac(e, bank=bank, oc=oc, tt=tt, dst=dst, bias=bias):
                        return e.activation(dst[:, oc, tt * 512:(tt + 1) * 512],
                                            P[bank][:], AF.Relu,
                                            bias=bias[:, oc:oc + 1])
                    mm_group(bank, P[bank], mms, "scalar", evac)
            done_wslice(par)

    proj_wstat(wkT, bkf, kown)
    k_sc = p.cur("s_sc")
    p.wait("gpsimd", "s_sc", k_sc)
    p.emit("gpsimd", lambda g: g.dma_start(
        out=cc_in_k.rearrange("(c p) t -> p c t", p=128),
        in_=kown[:].rearrange("p a b -> p (a b)").bitcast(F32)
            .rearrange("p (a b) -> p a b", a=DC)).then_inc(p.sems["s_ccin"], 16))
    p.inc("s_ccin", 16)
    p.wait("gpsimd", "s_ccin", p.cur("s_ccin"))
    p.emit("gpsimd", lambda g: g.collective_compute(
        "AllGather", ALU.bypass, replica_groups=RG,
        ins=[cc_in_k[0:1024, :]], outs=[cc_out_k[0][:, :]]).then_inc(p.sems["s_ag"], 1))
    p.inc("s_ag", 1)

    tap("t_k", kown[:].rearrange("p a b -> p (a b)"), "s_sc")
    for ws in range(4):
        par = ws % 2
        load_wslice(wvT, ws, par)
        use_wslice(par)
        for tc in range(8):
            bank = rot[0] % 2
            rot[0] += 1
            mms = [(lambda t, st, sp, par=par, i=i, tc=tc, bank=bank:
                    t.matmul(P[bank][:], x0[:, i, tc * 128:(tc + 1) * 128],
                             wb[par][:, i, :], start=st, stop=False))
                   for i in range(DC)]
            mms.append(lambda t, st, sp, ws=ws, bank=bank:
                       t.matmul(P[bank][:], ones_rb,
                                bvb[0:1, ws * 512:(ws + 1) * 512],
                                start=False, stop=True))
            def evac(e, bank=bank, tc=tc, ws=ws):
                return e.activation(vown[:, tc, ws * 512:(ws + 1) * 512],
                                    P[bank][:], AF.Relu)
            mm_group(bank, P[bank], mms, "scalar", evac)
        done_wslice(par)
    v_sc = p.cur("s_sc")
    p.wait("gpsimd", "s_sc", v_sc)
    vflat = vown[:].rearrange("p a b -> p (a b)").bitcast(F32) \
        .rearrange("p (a b) -> p a b", a=8)
    for h in range(2):
        p.emit("gpsimd", lambda g, h=h: g.dma_start(
            out=cc_in_v[h].rearrange("(c p) t -> p c t", p=128),
            in_=vflat[:, :, h * 512:(h + 1) * 512]).then_inc(p.sems["s_ccin"], 16))
        p.inc("s_ccin", 16)
    p.wait("gpsimd", "s_ccin", p.cur("s_ccin"))
    p.emit("gpsimd", lambda g: g.collective_compute(
        "AllGather", ALU.bypass, replica_groups=RG,
        ins=[cc_in_v[0][:, :]], outs=[cc_out_v[0][:, :]]).then_inc(p.sems["s_ag"], 1))
    p.inc("s_ag", 1)
    p.emit("gpsimd", lambda g: g.collective_compute(
        "AllGather", ALU.bypass, replica_groups=RG,
        ins=[cc_in_k[1024:2048, :]], outs=[cc_out_k[1][:, :]]).then_inc(p.sems["s_ag"], 1))
    p.inc("s_ag", 1)
    p.emit("gpsimd", lambda g: g.collective_compute(
        "AllGather", ALU.bypass, replica_groups=RG,
        ins=[cc_in_v[1][:, :]], outs=[cc_out_v[1][:, :]]).then_inc(p.sems["s_ag"], 1))
    p.inc("s_ag", 1)

    tap("t_v", vown[:].rearrange("p a b -> p (a b)"), "s_sc")
    proj_wstat(wqT, bqf, qf)
    tap("t_q", qf[:].rearrange("p a b -> p (a b)"), "s_sc")

    ccok = [cc_out_k[h].rearrange("(n p) c -> n p c", p=128) for h in range(2)]
    ccov = [cc_out_v[h].rearrange("(n p) c -> n p c", p=128) for h in range(2)]

    kb_mm = [0, 0]
    vb_mm = [0, 0]

    def load_kb(h, kc, par):
        p.wait("sync", "s_mm", kb_mm[par])
        r, j = kc // 8, kc % 8
        src = ccok[h][r * 8: r * 8 + 8, :, j * 64:(j + 1) * 64]
        p.emit("sync", lambda s, src=src, par=par: s.dma_start(
            out=kb[par][:].rearrange("p a b -> p (a b)").bitcast(F32)
                .rearrange("p (a b) -> p a b", a=8),
            in_=src.rearrange("a p b -> p a b")).then_inc(p.sems[f"s_kbl{par}"], 16))
        p.inc(f"s_kbl{par}", 16)

    def load_vb(h, kc, par):
        p.wait("sync", "s_mm", vb_mm[par])
        src = ccov[h][kc, :, :]
        p.emit("sync", lambda s, src=src, par=par: s.dma_start(
            out=vb[par][:].rearrange("p a b -> p (a b)").bitcast(F32),
            in_=src).then_inc(p.sems[f"s_vbl{par}"], 16))
        p.inc(f"s_vbl{par}", 16)

    p.wait("scalar", "s_ccin", p.cur("s_ccin"))

    for h in range(H):
        for qt in range(TT):
            if qt == 0:
                p.wait("sync", "s_ag", 2 * h + 1)
                p.wait("sync", "s_ag", 2 * h + 2)
            exp_sc = {}
            for hc in range(4):
                wait_bank(hc)
            wait_bank("PS1")
            for kc in range(33):
                if kc < 32:
                    par = kc % 2
                    load_kb(h, kc, par)
                    load_vb(h, kc, par)
                    sb_bank = 4 + (kc % 2)
                    wait_bank(sb_bank)
                    p.wait("tensor", f"s_kbl{par}", p.cur(f"s_kbl{par}"))
                    for hc in range(8):
                        last = hc == 7
                        p.emit("tensor", lambda t, b=sb_bank, par=par, hc=hc, h=h,
                               qt=qt, last=last:
                               (t.matmul(P[b][:], kb[par][:, hc, :],
                                         qf[:, 8 * h + hc, qt * 512:(qt + 1) * 512],
                                         start=(hc == 0), stop=last)
                                .then_inc(p.sems["s_mm"], 1)) if last else
                               t.matmul(P[b][:], kb[par][:, hc, :],
                                        qf[:, 8 * h + hc, qt * 512:(qt + 1) * 512],
                                        start=(hc == 0), stop=last))
                    mmv = p.inc("s_mm", 1)
                    kb_mm[par] = mmv
                    p.wait("scalar", "s_mm", mmv)
                    p.emit("scalar", lambda s, b=sb_bank, kc=kc:
                           s.activation(pview[:, kc, :], P[b][:], AF.Exp,
                                        bias=nshift, scale=SCALE)
                           .then_inc(p.sems["s_sc"], 1))
                    scv = p.inc("s_sc", 1)
                    exp_sc[kc] = scv
                    bank_free[sb_bank] = ("s_sc", scv)
                if kc >= 1:
                    kp = kc - 1
                    parp = kp % 2
                    p.wait("tensor", "s_sc", exp_sc[kp])
                    p.wait("tensor", f"s_vbl{parp}", p.cur(f"s_vbl{parp}"))
                    for hc in range(4):
                        p.emit("tensor", lambda t, hc=hc, parp=parp, kp=kp:
                               t.matmul(P[hc][:], vb[parp][:, hc, :], pview[:, kp, :],
                                        start=(kp == 0), stop=(kp == 31)))
                    p.emit("tensor", lambda t, kp=kp:
                           t.matmul(PS1[:, :], ones_c, pview[:, kp, :],
                                    start=(kp == 0), stop=(kp == 31))
                           .then_inc(p.sems["s_mm"], 1))
                    vb_mm[parp] = p.inc("s_mm", 1)
            mmv = p.cur("s_mm")
            p.wait("vector", "s_mm", mmv)
            p.emit("vector", lambda v: v.reciprocal(recd, PS1[:, :])
                   .then_inc(p.sems["s_ve"], 1))
            vev = p.inc("s_ve", 1)
            bank_free["PS1"] = ("s_ve", vev)
            p.wait("tensor", "s_ve", vev)
            wait_bank(4)
            p.emit("tensor", lambda t: t.matmul(P[4][:], ones_r, recd,
                                                start=True, stop=True)
                   .then_inc(p.sems["s_mm"], 1))
            mmv = p.inc("s_mm", 1)
            p.wait("vector", "s_mm", mmv)
            p.emit("vector", lambda v: v.tensor_copy(rrep, P[4][:])
                   .then_inc(p.sems["s_ve"], 1))
            rrv = p.inc("s_ve", 1)
            bank_free[4] = ("s_ve", rrv)
            p.wait("vector", "s_ve", rrv)
            for hc in range(4):
                p.emit("vector", lambda v, hc=hc, h=h, qt=qt:
                       v.tensor_mul(yf[:, 8 * h + hc, qt * 512:(qt + 1) * 512],
                                    P[hc][:], rrep).then_inc(p.sems["s_ve"], 1))
                vev = p.inc("s_ve", 1)
                bank_free[hc] = ("s_ve", vev)
            for kc in range(32):
                par = kc % 2
                load_vb(h, kc, par)
                p.wait("tensor", f"s_vbl{par}", p.cur(f"s_vbl{par}"))
                if kc == 0:
                    for hc in range(4):
                        wait_bank(hc)
                for hc in range(4):
                    p.emit("tensor", lambda t, hc=hc, par=par, kc=kc, last=(hc == 3):
                           (t.matmul(P[hc][:], vb[par][:, 4 + hc, :], pview[:, kc, :],
                                     start=(kc == 0), stop=(kc == 31))
                            .then_inc(p.sems["s_mm"], 1)) if last else
                           t.matmul(P[hc][:], vb[par][:, 4 + hc, :], pview[:, kc, :],
                                    start=(kc == 0), stop=(kc == 31)))
                vb_mm[par] = p.inc("s_mm", 1)
            mmv = p.cur("s_mm")
            p.wait("vector", "s_mm", mmv)
            for hc in range(4):
                p.emit("vector", lambda v, hc=hc, h=h, qt=qt:
                       v.tensor_mul(yf[:, 8 * h + 4 + hc, qt * 512:(qt + 1) * 512],
                                    P[hc][:], rrep).then_inc(p.sems["s_ve"], 1))
                vev = p.inc("s_ve", 1)
                bank_free[hc] = ("s_ve", vev)

    y_ve = p.cur("s_ve")
    tap("t_y", yf[:].rearrange("p a b -> p (a b)"), "s_ve")
    tap("t_p", pview[:].rearrange("p a b -> p (a b)"), "s_ve")

    at_guard = [0, 0]
    p.wait("tensor", "s_ve", y_ve)
    for ws in range(4):
        par = ws % 2
        load_wslice(woT, ws, par)
        use_wslice(par)
        for tt in range(TT):
            for oc2 in range(4):
                oc = ws * 4 + oc2
                bank = rot[0] % 2
                rot[0] += 1
                apar = (tt * 4 + oc2) % 2
                mms = [(lambda t, st, sp, par=par, i=i, oc2=oc2, tt=tt, bank=bank:
                        t.matmul(P[bank][:],
                                 wb[par][:, i, oc2 * 128:(oc2 + 1) * 128],
                                 yf[:, i, tt * 512:(tt + 1) * 512],
                                 start=st, stop=sp)) for i in range(DC)]
                p.wait("scalar", "s_ve", at_guard[apar])
                def evac(e, bank=bank, oc=oc, apar=apar):
                    return e.activation(atile[apar][:], P[bank][:], AF.Relu,
                                        bias=bof[:, oc:oc + 1])
                mmv, scv = mm_group(bank, P[bank], mms, "scalar", evac)
                p.wait("vector", "s_sc", scv)
                p.emit("vector", lambda v, oc=oc, tt=tt, apar=apar:
                       v.tensor_add(qf[:, oc, tt * 512:(tt + 1) * 512],
                                    atile[apar][:],
                                    x0[:, oc, tt * 512:(tt + 1) * 512])
                       .then_inc(p.sems["s_ve"], 1))
                at_guard[apar] = p.inc("s_ve", 1)
        done_wslice(par)
    z_ve = p.cur("s_ve")
    tap("t_z", qf[:].rearrange("p a b -> p (a b)"), "s_ve")

    def layernorm_fm(z_ap, xo_ap, width, f32src, ln_ready_ve):
        mean = stat[0:1, 0:512]
        msq = stat[0:1, 512:1024]
        var = stat[0:1, 1024:1536]
        rstd = stat[0:1, 1536:2048]
        nmr = stat[0:1, 2048:2560]
        ntt = z_ap.shape[2] // width
        for tt in range(ntt):
            zts = [z_ap[:, i, tt * width:(tt + 1) * width] for i in range(DC)]
            p.wait("scalar", "s_ve", ln_ready_ve)
            if tt > 0:
                p.wait("scalar", "s_mm", p.cur("s_mm"))
            for i in range(DC):
                p.emit("scalar", lambda s, i=i, z=zts[i]:
                       s.activation(zsq[:, i, :], z, AF.Square)
                       .then_inc(p.sems["s_sc"], 1))
                p.inc("s_sc", 1)
            sq_sc = p.cur("s_sc")
            p.wait("tensor", "s_sc", sq_sc)
            wait_bank("PS1")
            wait_bank("PS2")
            oz = ones_cf if f32src else ones_c
            for i in range(DC):
                last = i == DC - 1
                p.emit("tensor", lambda t, i=i, z=zts[i], oz=oz, last=last:
                       (t.matmul(PS1[:, :], oz, z, start=(i == 0), stop=last)
                        .then_inc(p.sems["s_mm"], 1)) if last else
                       t.matmul(PS1[:, :], oz, z, start=(i == 0), stop=last))
            p.inc("s_mm", 1)
            for i in range(DC):
                last = i == DC - 1
                p.emit("tensor", lambda t, i=i, last=last:
                       (t.matmul(PS2[:, :], ones_c, zsq[:, i, :],
                                 start=(i == 0), stop=last)
                        .then_inc(p.sems["s_mm"], 1)) if last else
                       t.matmul(PS2[:, :], ones_c, zsq[:, i, :],
                                start=(i == 0), stop=last))
            mmv = p.inc("s_mm", 1)
            p.wait("vector", "s_mm", mmv)
            p.emit("vector", lambda v: v.tensor_scalar(mean, PS1[:, :], 1.0 / D,
                                                       None, ALU.mult))
            p.emit("vector", lambda v: v.tensor_scalar(msq, PS2[:, :], 1.0 / D,
                                                       None, ALU.mult)
                   .then_inc(p.sems["s_ve"], 1))
            vev = p.inc("s_ve", 1)
            bank_free["PS1"] = ("s_ve", vev)
            bank_free["PS2"] = ("s_ve", vev)
            p.emit("vector", lambda v: v.scalar_tensor_tensor(
                var, mean, -1.0, mean, ALU.mult, ALU.mult))
            p.emit("vector", lambda v: v.tensor_add(var, var, msq))
            p.emit("vector", lambda v: v.tensor_scalar(
                var, var, 1.0, EPS, ALU.mult, ALU.add).then_inc(p.sems["s_ve"], 1))
            vev = p.inc("s_ve", 1)
            p.wait("scalar", "s_ve", vev)
            p.emit("scalar", lambda s: s.activation(rstd, var, AF.Sqrt)
                   .then_inc(p.sems["s_sc"], 1))
            scv = p.inc("s_sc", 1)
            p.wait("vector", "s_sc", scv)
            p.emit("vector", lambda v: v.reciprocal(rstd, rstd))
            p.emit("vector", lambda v: v.scalar_tensor_tensor(
                nmr, mean, -1.0, rstd, ALU.mult, ALU.mult).then_inc(p.sems["s_ve"], 1))
            vev = p.inc("s_ve", 1)
            p.wait("tensor", "s_ve", vev)
            wait_bank(4)
            wait_bank(5)
            p.emit("tensor", lambda t: t.matmul(P[4][:], ones_r, rstd,
                                                start=True, stop=True))
            p.emit("tensor", lambda t: t.matmul(P[5][:], ones_r, nmr,
                                                start=True, stop=True)
                   .then_inc(p.sems["s_mm"], 1))
            mmv = p.inc("s_mm", 1)
            p.wait("vector", "s_mm", mmv)
            p.emit("vector", lambda v: v.tensor_copy(rrep, P[4][:]))
            p.emit("vector", lambda v: v.tensor_copy(rrep2, P[5][:])
                   .then_inc(p.sems["s_ve"], 1))
            rv = p.inc("s_ve", 1)
            bank_free[4] = ("s_ve", rv)
            bank_free[5] = ("s_ve", rv)
            p.wait("vector", "s_ve", rv)
            for i in range(DC):
                p.emit("vector", lambda v, z=zts[i]:
                       v.tensor_mul(f32a, z, rrep))
                p.emit("vector", lambda v, i=i, tt=tt:
                       v.tensor_add(xo_ap[:, i, tt * width:(tt + 1) * width],
                                    f32a, rrep2))
            p.emit("vector", lambda v: v.memset(f32b[0:1, 0:1], 0.0)
                   .then_inc(p.sems["s_ve"], 1))
            p.inc("s_ve", 1)

    layernorm_fm(qf, x1, 512, False, z_ve)
    ln1_ve = p.cur("s_ve")
    p.wait("sync", "s_ve", ln1_ve)
    p.emit("sync", lambda s: s.dma_start(out=idf, in_=id_in)
           .then_inc(p.sems["s_in"], 16))
    IDF_IN = p.inc("s_in", 16)

    p.wait("tensor", "s_ve", ln1_ve)
    for tt in range(TT):
        wait_bank("PS1")
        for i in range(DC):
            last = i == DC - 1
            p.emit("tensor", lambda t, i=i, tt=tt, last=last:
                   (t.matmul(PS1[:, :], ones_c, x1[:, i, tt * 512:(tt + 1) * 512],
                             start=(i == 0), stop=last).then_inc(p.sems["s_mm"], 1))
                   if last else
                   t.matmul(PS1[:, :], ones_c, x1[:, i, tt * 512:(tt + 1) * 512],
                            start=(i == 0), stop=last))
        mmv = p.inc("s_mm", 1)
        p.wait("vector", "s_mm", mmv)
        p.emit("vector", lambda v, tt=tt: v.tensor_scalar(
            csm[0:1, tt * 512:(tt + 1) * 512], PS1[:, :], -128.0, None,
            ALU.mult).then_inc(p.sems["s_ve"], 1))
        vev = p.inc("s_ve", 1)
        bank_free["PS1"] = ("s_ve", vev)
    csm_ve = p.cur("s_ve")
    tap("t_x1", x1[:].rearrange("p a b -> p (a b)"), "s_ve")

    p.wait("tensor", "s_ve", csm_ve)
    for tt in range(TT):
        csm_t = csm[0:1, tt * 512:(tt + 1) * 512]
        for ws in range(11):
            load_wslice(wgT, ws, 0, u8=True)
            load_wslice(wuT, ws, 1, u8=True)
            use_wslice(0)
            use_wslice(1)
            for hc2 in range(4):
                hcg = ws * 4 + hc2
                bank_g = (2, 4)[hc2 % 2]
                bank_u = (3, 5)[hc2 % 2]
                for which, bank, wpar in (("g", bank_g, 0), ("u", bank_u, 1)):
                    mms = [(lambda t, st, sp, i=i, hc2=hc2, tt=tt, bank=bank, wpar=wpar:
                            t.matmul(P[bank][:],
                                     wb[wpar][:, i, hc2 * 128:(hc2 + 1) * 128],
                                     x1[:, i, tt * 512:(tt + 1) * 512],
                                     start=st, stop=False)) for i in range(DC)]
                    mms.append(lambda t, st, sp, bank=bank, csm_t=csm_t:
                               t.matmul(P[bank][:], ones_r, csm_t,
                                        start=False, stop=True))
                    scl = sgf if which == "g" else suf
                    dst = f32a if which == "g" else f32b
                    def evac(e, bank=bank, hcg=hcg, scl=scl, dst=dst):
                        return e.tensor_scalar(dst, P[bank][:],
                                               scl[:, hcg:hcg + 1], None, ALU.mult)
                    mm_group(bank, P[bank], mms, "vector", evac)
                uv = p.cur("s_ve")
                p.wait("scalar", "s_ve", uv)
                p.emit("scalar", lambda s: s.activation(f32a, f32a, AF.Silu)
                       .then_inc(p.sems["s_sc"], 1))
                scv = p.inc("s_sc", 1)
                p.wait("vector", "s_sc", scv)
                p.emit("vector", lambda v, hcg=hcg: v.tensor_mul(
                    harena[:, hcg, :], f32a, f32b).then_inc(p.sems["s_ve"], 1))
                p.inc("s_ve", 1)
            done_wslice(0)
            done_wslice(1)
        h_ve = p.cur("s_ve")

        if tt == 0:
            tap("t_h", harena[:].rearrange("p a b -> p (a b)"), "s_ve")
        p.wait("tensor", "s_ve", h_ve)
        wait_bank("PS1")
        for i in range(HC):
            last = i == HC - 1
            p.emit("tensor", lambda t, i=i, last=last:
                   (t.matmul(PS1[:, :], ones_c, harena[:, i, :],
                             start=(i == 0), stop=last).then_inc(p.sems["s_mm"], 1))
                   if last else
                   t.matmul(PS1[:, :], ones_c, harena[:, i, :],
                            start=(i == 0), stop=last))
        mmv = p.inc("s_mm", 1)
        p.wait("vector", "s_mm", mmv)
        p.emit("vector", lambda v: v.tensor_scalar(
            hsm, PS1[:, :], -128.0, None, ALU.mult).then_inc(p.sems["s_ve"], 1))
        hs_ve = p.inc("s_ve", 1)
        bank_free["PS1"] = ("s_ve", hs_ve)
        p.wait("tensor", "s_ve", hs_ve)

        for ws in range(16):
            par = ws % 2
            load_wslice(wdT, ws, par, u8=True, down=True)
            use_wslice(par)
            oc = ws
            bank = rot[0] % 2
            rot[0] += 1
            mms = [(lambda t, st, sp, i=i, par=par, bank=bank:
                    t.matmul(P[bank][:], wbd[par][:, i, :], harena[:, i, :],
                             start=st, stop=False)) for i in range(HC)]
            mms.append(lambda t, st, sp, bank=bank:
                       t.matmul(P[bank][:], ones_r, hsm, start=False, stop=True))
            def evac(e, bank=bank, oc=oc):
                return e.tensor_scalar(f32b, P[bank][:], sdf[:, oc:oc + 1],
                                       None, ALU.mult)
            mmv, vev = mm_group(bank, P[bank], mms, "vector", evac)
            p.emit("vector", lambda v, oc=oc, tt=tt: v.tensor_add(
                z2[:, oc, :], f32b,
                x1[:, oc, tt * 512:(tt + 1) * 512]).then_inc(p.sems["s_ve"], 1))
            p.inc("s_ve", 1)
            done_wslice(par)
        z2_ve = p.cur("s_ve")
        if tt == 0:
            tap("t_dn", z2[:].rearrange("p a b -> p (a b)"), "s_ve")

        layernorm_fm(z2, z2, 512, True, z2_ve)
        x2_ve = p.cur("s_ve")

        if tt == 0:
            tap("t_x2", z2[:].rearrange("p a b -> p (a b)"), "s_ve")
        p.wait("tensor", "s_ve", x2_ve)
        p.wait("tensor", "s_in", IDF_IN)
        for tc2 in range(4):
            for i in range(DC):
                bank = 4 + (i % 2)
                wait_bank(bank)
                pt = P[bank][:, 0:128]
                p.emit("tensor", lambda t, pt=pt, i=i, tc2=tc2:
                       t.transpose(pt, z2[:, i, tc2 * 128:(tc2 + 1) * 128], idf)
                       .then_inc(p.sems["s_mm"], 1))
                mmv = p.inc("s_mm", 1)
                p.wait("vector", "s_mm", mmv)
                p.emit("vector", lambda v, pt=pt, i=i: v.tensor_copy(
                    otile[:, i * 128:(i + 1) * 128], pt).then_inc(p.sems["s_ve"], 1))
                vev = p.inc("s_ve", 1)
                bank_free[bank] = ("s_ve", vev)
            ot_ve = p.cur("s_ve")
            p.wait("sync", "s_ve", ot_ve)
            row = tt * 512 + tc2 * 128
            p.emit("sync", lambda s, row=row: s.dma_start(
                out=out[row:row + 128, :], in_=otile).then_inc(p.sems["s_out"], 16))
            p.inc("s_out", 16)
            p.wait("vector", "s_out", p.cur("s_out"))

    p.wait("sync", "s_out", p.cur("s_out"))

    with nc.Block() as block:
        @block.sync
        def _(sync):
            for fn in p.ops["sync"]:
                fn(sync)

        @block.gpsimd
        def _(g):
            for fn in p.ops["gpsimd"]:
                fn(g)

        @block.tensor
        def _(t):
            for fn in p.ops["tensor"]:
                fn(t)

        @block.vector
        def _(v):
            for fn in p.ops["vector"]:
                fn(v)

        @block.scalar
        def _(s):
            for fn in p.ops["scalar"]:
                fn(s)

    return nc


def _u8_low_byte(a_int32):
    v = a_int32.view(np.uint8).reshape(*a_int32.shape, 4)
    return np.ascontiguousarray(v[..., 0] if np.little_endian else v[..., 3])


_NC_CACHE = {}


def kernel(L0_attn_base, L0_attn_Wq, L0_attn_bq, L0_attn_Wk, L0_attn_bk,
           L0_attn_Wv, L0_attn_bv, L0_attn_Wo, L0_attn_bo,
           L0_ffn_wq_gate, L0_ffn_s_gate, L0_ffn_wq_up, L0_ffn_s_up,
           L0_ffn_wq_down, L0_ffn_s_down):
    bf16 = ml_dtypes.bfloat16
    base = np.asarray(L0_attn_base)
    assert base.dtype == bf16

    def pcol(b, nch):
        return np.ascontiguousarray(
            np.asarray(b, dtype=np.float32).reshape(nch, 128).T)

    shared = {
        "wqT": np.ascontiguousarray(np.asarray(L0_attn_Wq).T),
        "wkT": np.ascontiguousarray(np.asarray(L0_attn_Wk).T),
        "wvT": np.ascontiguousarray(np.asarray(L0_attn_Wv).T),
        "woT": np.ascontiguousarray(np.asarray(L0_attn_Wo).T),
        "bq": pcol(L0_attn_bq, DC),
        "bk": pcol(L0_attn_bk, DC),
        "bo": pcol(L0_attn_bo, DC),
        "bv": np.ascontiguousarray(np.asarray(L0_attn_bv)[None, :]),
        "wgT": np.ascontiguousarray(_u8_low_byte(np.asarray(L0_ffn_wq_gate)).T),
        "wuT": np.ascontiguousarray(_u8_low_byte(np.asarray(L0_ffn_wq_up)).T),
        "wdT": np.ascontiguousarray(_u8_low_byte(np.asarray(L0_ffn_wq_down)).T),
        "sg": pcol(L0_ffn_s_gate, HC),
        "su": pcol(L0_ffn_s_up, HC),
        "sd": pcol(L0_ffn_s_down, DC),
        "idf": np.eye(128, dtype=np.float32),
    }

    in_maps = []
    for c in range(8):
        b, q = c // 4, c % 4
        m = dict(shared)
        m["x0fm"] = np.ascontiguousarray(base[b, q * T:(q + 1) * T, :].T)
        in_maps.append(m)

    nc = build_nc()
    res = run_bass_kernel_spmd(nc, in_maps, core_ids=list(range(8)))

    outv = np.empty((B, S, D), dtype=np.float32)
    for c in range(8):
        b, q = c // 4, c % 4
        outv[b, q * T:(q + 1) * T, :] = np.asarray(res.results[c]["out"])
    return outv


# version 03
# speedup vs baseline: 1.2058x; 1.2058x over previous
import numpy as np
import ml_dtypes

import concourse.bass as bass
import concourse.mybir as mybir
from concourse.bass_utils import run_bass_kernel_spmd

BF16 = mybir.dt.bfloat16
F32 = mybir.dt.float32
U8 = mybir.dt.uint8
AF = mybir.ActivationFunctionType
ALU = mybir.AluOpType

B, S, D = 2, 4096, 2048
T = 1024
H = 2
HID = 5632
DC = D // 128
HC = HID // 128
TT = T // 512
SHIFT = 6.0
SCALE = 1.0 / 32.0
EPS = 1e-5


class Prog:
    ENGINES = ("sync", "gpsimd", "tensor", "vector", "scalar")

    def __init__(self, nc):
        self.nc = nc
        self.ops = {e: [] for e in self.ENGINES}
        self.sems = {}
        self.cnt = {}

    def sem(self, name):
        h = self.nc.semaphore(name).__enter__()
        self.sems[name] = h
        self.cnt[name] = 0

    def inc(self, name, amt=1):
        self.cnt[name] += amt
        return self.cnt[name]

    def cur(self, name):
        return self.cnt[name]

    def emit(self, eng, fn):
        self.ops[eng].append(fn)

    def wait(self, eng, name, val):
        if val <= 0:
            return
        h = self.sems[name]
        self.emit(eng, lambda e, h=h, v=val: e.wait_ge(h, v))


def build_nc(taps=()):
    nc = bass.Bass("TRN2", target_bir_lowering=False, debug=False)

    x0_in = nc.dram_tensor("x0fm", [D, T], BF16, kind="ExternalInput").ap()
    wqT = nc.dram_tensor("wqT", [D, D], BF16, kind="ExternalInput").ap()
    wkT = nc.dram_tensor("wkT", [D, D], BF16, kind="ExternalInput").ap()
    wvT = nc.dram_tensor("wvT", [D, D], BF16, kind="ExternalInput").ap()
    woT = nc.dram_tensor("woT", [D, D], BF16, kind="ExternalInput").ap()
    bq_in = nc.dram_tensor("bq", [128, DC], F32, kind="ExternalInput").ap()
    bk_in = nc.dram_tensor("bk", [128, DC], F32, kind="ExternalInput").ap()
    bo_in = nc.dram_tensor("bo", [128, DC], F32, kind="ExternalInput").ap()
    bv_in = nc.dram_tensor("bv", [1, D], BF16, kind="ExternalInput").ap()
    wgT = nc.dram_tensor("wgT", [22, 128, DC * 256], U8, kind="ExternalInput").ap()
    wuT = nc.dram_tensor("wuT", [22, 128, DC * 256], U8, kind="ExternalInput").ap()
    wdT = nc.dram_tensor("wdT", [16, 128, HC * 128], U8, kind="ExternalInput").ap()
    sg_in = nc.dram_tensor("sg", [128, HC], F32, kind="ExternalInput").ap()
    su_in = nc.dram_tensor("su", [128, HC], F32, kind="ExternalInput").ap()
    sd_in = nc.dram_tensor("sd", [128, DC], F32, kind="ExternalInput").ap()
    id_in = nc.dram_tensor("idf", [128, 128], F32, kind="ExternalInput").ap()

    out = nc.dram_tensor("out", [T, D], F32, kind="ExternalOutput").ap()
    tap_t = {}
    for tn, shape, dt in (("t_k", [128, DC * T], BF16), ("t_v", [128, 8 * D], BF16),
                          ("t_q", [128, DC * T], BF16), ("t_y", [128, DC * T], BF16),
                          ("t_z", [128, DC * T], BF16), ("t_x1", [128, DC * T], BF16),
                          ("t_h", [128, HC * 512], BF16), ("t_dn", [128, DC * 512], F32),
                          ("t_x2", [128, DC * 512], F32),
                          ("t_p", [128, 32 * 512], BF16)):
        if tn in taps:
            tap_t[tn] = nc.dram_tensor(tn, shape, dt, kind="ExternalOutput").ap()

    cc_in_k = nc.dram_tensor("cc_in_k", [D, T // 2], F32, kind="Internal").ap()
    cc_out_k = [nc.dram_tensor(f"cc_out_k{h}", [4 * D // 2, T // 2], F32,
                               kind="Internal").ap() for h in range(2)]
    cc_in_v = [nc.dram_tensor(f"cc_in_v{h}", [T, D // 4], F32,
                              kind="Internal").ap() for h in range(2)]
    cc_out_v = [nc.dram_tensor(f"cc_out_v{h}", [4 * T, D // 4], F32,
                               kind="Internal").ap() for h in range(2)]
    RG = [[0, 1, 2, 3], [4, 5, 6, 7]]

    AVAIL = nc.sbuf_bytes_remaining
    NB = AVAIL // 256 * 256
    arena = nc.alloc_sbuf_tensor("arena", [128, NB], U8)

    def carve(off, shape, dtype, nparts=128):
        es = mybir.dt.size(dtype)
        n = es
        for s in shape[1:]:
            n *= s
        assert off % 64 == 0 and off + n <= NB, (off, n, NB)
        ap = arena[0:nparts, off:off + n].bitcast(dtype)
        if len(shape) == 2:
            return ap
        if len(shape) == 3:
            return ap.rearrange("p (a b) -> p a b", a=shape[1])
        raise AssertionError(shape)

    KB = 1024
    x0 = carve(0, [128, DC, T], BF16)
    harena = carve(0, [128, HC, 512], BF16)
    zarena = carve(32 * KB, [128, DC, T], BF16)
    qf = zarena
    otile = carve(44 * KB, [128, D], F32)
    csm = carve(52 * KB, [1, T], F32, 1)
    hsm = carve(56 * KB, [1, 512], F32, 1)
    x1 = carve(64 * KB, [128, DC, T], BF16)
    kown = x1
    vown = carve(96 * KB, [128, 8, D], BF16)
    pview = carve(96 * KB, [128, 32, 512], BF16)
    zsq = carve(96 * KB, [128, DC, 512], BF16)
    f32a = carve(112 * KB, [128, 512], F32)
    f32b = carve(114 * KB, [128, 512], F32)
    atile = [carve(116 * KB, [128, 512], BF16), carve(117 * KB, [128, 512], BF16)]
    stat = carve(118 * KB, [1, 5 * 512], F32, 1)
    idf = carve(58 * KB, [128, 128], F32)
    yf = carve(128 * KB, [128, DC, T], BF16)
    z2 = carve(128 * KB, [128, DC, 512], F32)
    bvb = carve(128 * KB, [1, D], BF16, 1)
    wb = [carve(160 * KB, [128, DC, 512], BF16),
          carve(176 * KB, [128, DC, 512], BF16)]
    wgb = [carve(160 * KB, [128, DC, 256], BF16), carve(168 * KB, [128, DC, 256], BF16)]
    wub = [carve(176 * KB, [128, DC, 256], BF16), carve(184 * KB, [128, DC, 256], BF16)]
    wbd = [carve(160 * KB, [128, HC, 128], BF16),
           carve(176 * KB, [128, HC, 128], BF16)]
    kb = [carve(192 * KB, [128, 8, 128], BF16), carve(194 * KB, [128, 8, 128], BF16),
          carve(64 * KB, [128, 8, 128], BF16), carve(66 * KB, [128, 8, 128], BF16)]
    vb = [carve(196 * KB, [128, 8, 128], BF16), carve(198 * KB, [128, 8, 128], BF16),
          carve(68 * KB, [128, 8, 128], BF16), carve(70 * KB, [128, 8, 128], BF16)]
    recd = carve(200 * KB, [1, 512], F32, 1)
    rrep = carve(202 * KB, [128, 512], F32)
    rrep2 = carve(204 * KB, [128, 512], F32)
    sm = 206 * KB
    bqf = carve(sm, [128, DC], F32); sm += 64
    bkf = carve(sm, [128, DC], F32); sm += 64
    bof = carve(sm, [128, DC], F32); sm += 64
    sdf = carve(sm, [128, DC], F32); sm += 64
    sgf = carve(sm, [128, HC], F32); sm += 192
    suf = carve(sm, [128, HC], F32); sm += 192
    ones_c = carve(sm, [128, 1], BF16); sm += 64
    ones_cf = carve(sm, [128, 1], F32); sm += 64
    ones_rb = carve(sm, [1, 128], BF16, 1); sm += 256
    ones_r = carve(sm, [1, 128], F32, 1); sm += 512
    nshift = carve(sm, [128, 1], F32); sm += 64
    assert sm <= NB

    P = [nc.alloc_psum_tensor(f"P{i}", [128, 512], F32) for i in range(6)]
    PS1 = nc.alloc_psum_tensor("PS1", [1, 512], F32)
    PS2 = nc.alloc_psum_tensor("PS2", [1, 512], F32)

    p = Prog(nc)
    for s in ("s_in", "s_ccin", "s_ag", "s_mm", "s_sc", "s_ve", "s_out",
              "s_wl0", "s_wl1", "s_wu0", "s_wu1",
              "s_kbl0", "s_kbl1", "s_kbl2", "s_kbl3",
              "s_vbl0", "s_vbl1", "s_vbl2", "s_vbl3",
              "s_wgl0", "s_wgl1", "s_wul0", "s_wul1"):
        p.sem(s)

    def tap(tn, src_ap, sem_name):
        if tn not in tap_t:
            return
        p.wait("sync", sem_name, p.cur(sem_name))
        p.emit("sync", lambda s, tn=tn, a=src_ap: s.dma_start(
            out=tap_t[tn], in_=a).then_inc(p.sems["s_out"], 16))
        p.inc("s_out", 16)

    bank_free = {i: ("s_sc", 0) for i in list(range(6)) + ["PS1", "PS2"]}

    def wait_bank(bank, eng="tensor"):
        sname, val = bank_free[bank]
        p.wait(eng, sname, val)

    wl_cnt = [0, 0]
    wu_mm = [0, 0]

    def load_wslice(src_ap, ws, par, u8=False, down=False):
        eng = "gpsimd" if u8 else "sync"
        p.wait(eng, "s_mm", wu_mm[par])
        if down:
            dst = wbd[par][:]
            src = src_ap.rearrange("(c p) o -> p c o", p=128)[:, :, ws * 128:(ws + 1) * 128]
        else:
            dst = wb[par][:]
            src = src_ap.rearrange("(c p) o -> p c o", p=128)[:, :, ws * 512:(ws + 1) * 512]
        p.emit(eng, lambda e, dst=dst, src=src, par=par: e.dma_start(
            out=dst, in_=src).then_inc(p.sems[f"s_wl{par}"], 16))
        p.inc(f"s_wl{par}", 16)

    def use_wslice(par):
        p.wait("tensor", f"s_wl{par}", p.cur(f"s_wl{par}"))

    def done_wslice(par, last_mm_inst=None):
        wu_mm[par] = p.cur("s_mm")

    def mm_group(bank, psum_ap, mms, evac_eng, evac_fn):
        wait_bank(bank)
        n = len(mms)
        for j, emitfn in enumerate(mms):
            inc = (j == n - 1)
            p.emit("tensor", lambda t, f=emitfn, j=j, inc=inc:
                   (f(t, j == 0, j == n - 1).then_inc(p.sems["s_mm"], 1)
                    if inc else f(t, j == 0, j == n - 1)))
        mmv = p.inc("s_mm", 1)
        p.wait(evac_eng, "s_mm", mmv)
        sem = "s_sc" if evac_eng == "scalar" else "s_ve"
        p.emit(evac_eng, lambda e, f=evac_fn, sem=sem: f(e).then_inc(p.sems[sem], 1))
        ev = p.inc(sem, 1)
        bank_free[bank] = (sem, ev)
        return mmv, ev

    loads_sync = [(bvb, bv_in), (sgf, sg_in), (suf, su_in), (sdf, sd_in),
                  (bqf, bq_in), (bkf, bk_in), (bof, bo_in)]
    for t, src in loads_sync:
        p.emit("sync", lambda s, t=t, src=src: s.dma_start(
            out=t, in_=src).then_inc(p.sems["s_in"], 16))
        p.inc("s_in", 16)
    p.emit("sync", lambda s: s.dma_start(
        out=x0, in_=x0_in.rearrange("(c p) t -> p c t", p=128)
    ).then_inc(p.sems["s_in"], 16))
    p.inc("s_in", 16)
    SETUP_IN = p.cur("s_in")
    p.emit("vector", lambda v: v.memset(ones_c, 1.0))
    p.emit("vector", lambda v: v.memset(ones_cf, 1.0))
    p.emit("vector", lambda v: v.memset(ones_rb, 1.0))
    p.emit("vector", lambda v: v.memset(ones_r, 1.0))
    p.emit("vector", lambda v: v.memset(nshift, -SHIFT).then_inc(p.sems["s_ve"], 1))
    CONST_VE = p.inc("s_ve", 1)

    p.wait("tensor", "s_in", SETUP_IN)
    p.wait("tensor", "s_ve", CONST_VE)
    p.wait("scalar", "s_in", SETUP_IN)

    rot = [0]

    def proj_wstat(wsrc, bias, dst):
        for ws in range(4):
            par = ws % 2
            load_wslice(wsrc, ws, par)
            use_wslice(par)
            for tt in range(TT):
                for oc2 in range(4):
                    oc = ws * 4 + oc2
                    bank = rot[0] % 2
                    rot[0] += 1
                    mms = [
                        (lambda t, st, sp, par=par, i=i, oc2=oc2, tt=tt, bank=bank:
                         t.matmul(P[bank][:],
                                  wb[par][:, i, oc2 * 128:(oc2 + 1) * 128],
                                  qf[:, i, tt * 512:(tt + 1) * 512] if False else
                                  x0[:, i, tt * 512:(tt + 1) * 512],
                                  start=st, stop=sp))
                        for i in range(DC)]
                    def evac(e, bank=bank, oc=oc, tt=tt, dst=dst, bias=bias):
                        return e.activation(dst[:, oc, tt * 512:(tt + 1) * 512],
                                            P[bank][:], AF.Relu,
                                            bias=bias[:, oc:oc + 1])
                    mm_group(bank, P[bank], mms, "scalar", evac)
            done_wslice(par)

    proj_wstat(wkT, bkf, kown)
    k_sc = p.cur("s_sc")
    p.wait("gpsimd", "s_sc", k_sc)
    p.emit("gpsimd", lambda g: g.dma_start(
        out=cc_in_k.rearrange("(c p) t -> p c t", p=128),
        in_=kown[:].rearrange("p a b -> p (a b)").bitcast(F32)
            .rearrange("p (a b) -> p a b", a=DC)).then_inc(p.sems["s_ccin"], 16))
    p.inc("s_ccin", 16)
    p.wait("gpsimd", "s_ccin", p.cur("s_ccin"))
    p.emit("gpsimd", lambda g: g.collective_compute(
        "AllGather", ALU.bypass, replica_groups=RG,
        ins=[cc_in_k[0:1024, :]], outs=[cc_out_k[0][:, :]]).then_inc(p.sems["s_ag"], 1))
    p.inc("s_ag", 1)

    tap("t_k", kown[:].rearrange("p a b -> p (a b)"), "s_sc")
    for ws in range(4):
        par = ws % 2
        load_wslice(wvT, ws, par)
        use_wslice(par)
        for tc in range(8):
            bank = rot[0] % 2
            rot[0] += 1
            mms = [(lambda t, st, sp, par=par, i=i, tc=tc, bank=bank:
                    t.matmul(P[bank][:], x0[:, i, tc * 128:(tc + 1) * 128],
                             wb[par][:, i, :], start=st, stop=False))
                   for i in range(DC)]
            mms.append(lambda t, st, sp, ws=ws, bank=bank:
                       t.matmul(P[bank][:], ones_rb,
                                bvb[0:1, ws * 512:(ws + 1) * 512],
                                start=False, stop=True))
            def evac(e, bank=bank, tc=tc, ws=ws):
                return e.activation(vown[:, tc, ws * 512:(ws + 1) * 512],
                                    P[bank][:], AF.Relu)
            mm_group(bank, P[bank], mms, "scalar", evac)
        done_wslice(par)
    v_sc = p.cur("s_sc")
    p.wait("gpsimd", "s_sc", v_sc)
    vflat = vown[:].rearrange("p a b -> p (a b)").bitcast(F32) \
        .rearrange("p (a b) -> p a b", a=8)
    for h in range(2):
        p.emit("gpsimd", lambda g, h=h: g.dma_start(
            out=cc_in_v[h].rearrange("(c p) t -> p c t", p=128),
            in_=vflat[:, :, h * 512:(h + 1) * 512]).then_inc(p.sems["s_ccin"], 16))
        p.inc("s_ccin", 16)
    p.wait("gpsimd", "s_ccin", p.cur("s_ccin"))
    p.emit("gpsimd", lambda g: g.collective_compute(
        "AllGather", ALU.bypass, replica_groups=RG,
        ins=[cc_in_v[0][:, :]], outs=[cc_out_v[0][:, :]]).then_inc(p.sems["s_ag"], 1))
    p.inc("s_ag", 1)
    p.emit("gpsimd", lambda g: g.collective_compute(
        "AllGather", ALU.bypass, replica_groups=RG,
        ins=[cc_in_k[1024:2048, :]], outs=[cc_out_k[1][:, :]]).then_inc(p.sems["s_ag"], 1))
    p.inc("s_ag", 1)
    p.emit("gpsimd", lambda g: g.collective_compute(
        "AllGather", ALU.bypass, replica_groups=RG,
        ins=[cc_in_v[1][:, :]], outs=[cc_out_v[1][:, :]]).then_inc(p.sems["s_ag"], 1))
    p.inc("s_ag", 1)

    tap("t_v", vown[:].rearrange("p a b -> p (a b)"), "s_sc")
    proj_wstat(wqT, bqf, qf)
    tap("t_q", qf[:].rearrange("p a b -> p (a b)"), "s_sc")

    ccok = [cc_out_k[h].rearrange("(n p) c -> n p c", p=128) for h in range(2)]
    ccov = [cc_out_v[h].rearrange("(n p) c -> n p c", p=128) for h in range(2)]

    kb_mm = [0, 0, 0, 0]
    vb_mm = [0, 0, 0, 0]

    def load_kb(h, kc, par):
        p.wait("sync", "s_mm", kb_mm[par])
        r, j = kc // 8, kc % 8
        src = ccok[h][r * 8: r * 8 + 8, :, j * 64:(j + 1) * 64]
        p.emit("sync", lambda s, src=src, par=par: s.dma_start(
            out=kb[par][:].rearrange("p a b -> p (a b)").bitcast(F32)
                .rearrange("p (a b) -> p a b", a=8),
            in_=src.rearrange("a p b -> p a b")).then_inc(p.sems[f"s_kbl{par}"], 16))
        p.inc(f"s_kbl{par}", 16)

    def load_vb(h, kc, par):
        p.wait("sync", "s_mm", vb_mm[par])
        src = ccov[h][kc, :, :]
        p.emit("sync", lambda s, src=src, par=par: s.dma_start(
            out=vb[par][:].rearrange("p a b -> p (a b)").bitcast(F32),
            in_=src).then_inc(p.sems[f"s_vbl{par}"], 16))
        p.inc(f"s_vbl{par}", 16)

    p.wait("scalar", "s_ccin", p.cur("s_ccin"))

    for h in range(H):
        for qt in range(TT):
            if qt == 0:
                p.wait("sync", "s_ag", 2 * h + 1)
                p.wait("sync", "s_ag", 2 * h + 2)
            exp_sc = {}
            for hc in range(4):
                wait_bank(hc)
            wait_bank("PS1")
            for kc in range(33):
                if kc < 32:
                    par = kc % 4
                    load_kb(h, kc, par)
                    load_vb(h, kc, par)
                    sb_bank = 4 + (kc % 2)
                    wait_bank(sb_bank)
                    p.wait("tensor", f"s_kbl{par}", p.cur(f"s_kbl{par}"))
                    for hc in range(8):
                        last = hc == 7
                        p.emit("tensor", lambda t, b=sb_bank, par=par, hc=hc, h=h,
                               qt=qt, last=last:
                               (t.matmul(P[b][:], kb[par][:, hc, :],
                                         qf[:, 8 * h + hc, qt * 512:(qt + 1) * 512],
                                         start=(hc == 0), stop=last)
                                .then_inc(p.sems["s_mm"], 1)) if last else
                               t.matmul(P[b][:], kb[par][:, hc, :],
                                        qf[:, 8 * h + hc, qt * 512:(qt + 1) * 512],
                                        start=(hc == 0), stop=last))
                    mmv = p.inc("s_mm", 1)
                    kb_mm[par] = mmv
                    p.wait("scalar", "s_mm", mmv)
                    p.emit("scalar", lambda s, b=sb_bank, kc=kc:
                           s.activation(pview[:, kc, :], P[b][:], AF.Exp,
                                        bias=nshift, scale=SCALE)
                           .then_inc(p.sems["s_sc"], 1))
                    scv = p.inc("s_sc", 1)
                    exp_sc[kc] = scv
                    bank_free[sb_bank] = ("s_sc", scv)
                if kc >= 1:
                    kp = kc - 1
                    parp = kp % 4
                    p.wait("tensor", "s_sc", exp_sc[kp])
                    p.wait("tensor", f"s_vbl{parp}", p.cur(f"s_vbl{parp}"))
                    for hc in range(4):
                        p.emit("tensor", lambda t, hc=hc, parp=parp, kp=kp:
                               t.matmul(P[hc][:], vb[parp][:, hc, :], pview[:, kp, :],
                                        start=(kp == 0), stop=(kp == 31)))
                    p.emit("tensor", lambda t, kp=kp:
                           t.matmul(PS1[:, :], ones_c, pview[:, kp, :],
                                    start=(kp == 0), stop=(kp == 31))
                           .then_inc(p.sems["s_mm"], 1))
                    vb_mm[parp] = p.inc("s_mm", 1)
            mmv = p.cur("s_mm")
            p.wait("vector", "s_mm", mmv)
            p.emit("vector", lambda v: v.reciprocal(recd, PS1[:, :])
                   .then_inc(p.sems["s_ve"], 1))
            vev = p.inc("s_ve", 1)
            bank_free["PS1"] = ("s_ve", vev)
            p.wait("tensor", "s_ve", vev)
            wait_bank(4)
            p.emit("tensor", lambda t: t.matmul(P[4][:], ones_r, recd,
                                                start=True, stop=True)
                   .then_inc(p.sems["s_mm"], 1))
            mmv = p.inc("s_mm", 1)
            p.wait("vector", "s_mm", mmv)
            p.emit("vector", lambda v: v.tensor_copy(rrep, P[4][:])
                   .then_inc(p.sems["s_ve"], 1))
            rrv = p.inc("s_ve", 1)
            bank_free[4] = ("s_ve", rrv)
            p.wait("vector", "s_ve", rrv)
            for hc in range(4):
                p.emit("vector", lambda v, hc=hc, h=h, qt=qt:
                       v.tensor_mul(yf[:, 8 * h + hc, qt * 512:(qt + 1) * 512],
                                    P[hc][:], rrep).then_inc(p.sems["s_ve"], 1))
                vev = p.inc("s_ve", 1)
                bank_free[hc] = ("s_ve", vev)
            for kc in range(32):
                par = kc % 4
                load_vb(h, kc, par)
                p.wait("tensor", f"s_vbl{par}", p.cur(f"s_vbl{par}"))
                if kc == 0:
                    for hc in range(4):
                        wait_bank(hc)
                for hc in range(4):
                    p.emit("tensor", lambda t, hc=hc, par=par, kc=kc, last=(hc == 3):
                           (t.matmul(P[hc][:], vb[par][:, 4 + hc, :], pview[:, kc, :],
                                     start=(kc == 0), stop=(kc == 31))
                            .then_inc(p.sems["s_mm"], 1)) if last else
                           t.matmul(P[hc][:], vb[par][:, 4 + hc, :], pview[:, kc, :],
                                    start=(kc == 0), stop=(kc == 31)))
                vb_mm[par] = p.inc("s_mm", 1)
            mmv = p.cur("s_mm")
            p.wait("vector", "s_mm", mmv)
            for hc in range(4):
                p.emit("vector", lambda v, hc=hc, h=h, qt=qt:
                       v.tensor_mul(yf[:, 8 * h + 4 + hc, qt * 512:(qt + 1) * 512],
                                    P[hc][:], rrep).then_inc(p.sems["s_ve"], 1))
                vev = p.inc("s_ve", 1)
                bank_free[hc] = ("s_ve", vev)

    y_ve = p.cur("s_ve")
    tap("t_y", yf[:].rearrange("p a b -> p (a b)"), "s_ve")
    tap("t_p", pview[:].rearrange("p a b -> p (a b)"), "s_ve")

    at_guard = [0, 0]
    p.wait("tensor", "s_ve", y_ve)
    for ws in range(4):
        par = ws % 2
        load_wslice(woT, ws, par)
        use_wslice(par)
        for tt in range(TT):
            for oc2 in range(4):
                oc = ws * 4 + oc2
                bank = rot[0] % 2
                rot[0] += 1
                apar = (tt * 4 + oc2) % 2
                mms = [(lambda t, st, sp, par=par, i=i, oc2=oc2, tt=tt, bank=bank:
                        t.matmul(P[bank][:],
                                 wb[par][:, i, oc2 * 128:(oc2 + 1) * 128],
                                 yf[:, i, tt * 512:(tt + 1) * 512],
                                 start=st, stop=sp)) for i in range(DC)]
                p.wait("scalar", "s_ve", at_guard[apar])
                def evac(e, bank=bank, oc=oc, apar=apar):
                    return e.activation(atile[apar][:], P[bank][:], AF.Relu,
                                        bias=bof[:, oc:oc + 1])
                mmv, scv = mm_group(bank, P[bank], mms, "scalar", evac)
                p.wait("vector", "s_sc", scv)
                p.emit("vector", lambda v, oc=oc, tt=tt, apar=apar:
                       v.tensor_add(qf[:, oc, tt * 512:(tt + 1) * 512],
                                    atile[apar][:],
                                    x0[:, oc, tt * 512:(tt + 1) * 512])
                       .then_inc(p.sems["s_ve"], 1))
                at_guard[apar] = p.inc("s_ve", 1)
        done_wslice(par)
    z_ve = p.cur("s_ve")
    tap("t_z", qf[:].rearrange("p a b -> p (a b)"), "s_ve")

    def layernorm_fm(z_ap, xo_ap, width, f32src, ln_ready_ve):
        mean = stat[0:1, 0:512]
        msq = stat[0:1, 512:1024]
        var = stat[0:1, 1024:1536]
        rstd = stat[0:1, 1536:2048]
        nmr = stat[0:1, 2048:2560]
        ntt = z_ap.shape[2] // width
        for tt in range(ntt):
            zts = [z_ap[:, i, tt * width:(tt + 1) * width] for i in range(DC)]
            p.wait("scalar", "s_ve", ln_ready_ve)
            if tt > 0:
                p.wait("scalar", "s_mm", p.cur("s_mm"))
            for i in range(DC):
                p.emit("scalar", lambda s, i=i, z=zts[i]:
                       s.activation(zsq[:, i, :], z, AF.Square)
                       .then_inc(p.sems["s_sc"], 1))
                p.inc("s_sc", 1)
            sq_sc = p.cur("s_sc")
            p.wait("tensor", "s_sc", sq_sc)
            wait_bank("PS1")
            wait_bank("PS2")
            oz = ones_cf if f32src else ones_c
            for i in range(DC):
                last = i == DC - 1
                p.emit("tensor", lambda t, i=i, z=zts[i], oz=oz, last=last:
                       (t.matmul(PS1[:, :], oz, z, start=(i == 0), stop=last)
                        .then_inc(p.sems["s_mm"], 1)) if last else
                       t.matmul(PS1[:, :], oz, z, start=(i == 0), stop=last))
            p.inc("s_mm", 1)
            for i in range(DC):
                last = i == DC - 1
                p.emit("tensor", lambda t, i=i, last=last:
                       (t.matmul(PS2[:, :], ones_c, zsq[:, i, :],
                                 start=(i == 0), stop=last)
                        .then_inc(p.sems["s_mm"], 1)) if last else
                       t.matmul(PS2[:, :], ones_c, zsq[:, i, :],
                                start=(i == 0), stop=last))
            mmv = p.inc("s_mm", 1)
            p.wait("vector", "s_mm", mmv)
            p.emit("vector", lambda v: v.tensor_scalar(mean, PS1[:, :], 1.0 / D,
                                                       None, ALU.mult))
            p.emit("vector", lambda v: v.tensor_scalar(msq, PS2[:, :], 1.0 / D,
                                                       None, ALU.mult)
                   .then_inc(p.sems["s_ve"], 1))
            vev = p.inc("s_ve", 1)
            bank_free["PS1"] = ("s_ve", vev)
            bank_free["PS2"] = ("s_ve", vev)
            p.emit("vector", lambda v: v.scalar_tensor_tensor(
                var, mean, -1.0, mean, ALU.mult, ALU.mult))
            p.emit("vector", lambda v: v.tensor_add(var, var, msq))
            p.emit("vector", lambda v: v.tensor_scalar(
                var, var, 1.0, EPS, ALU.mult, ALU.add).then_inc(p.sems["s_ve"], 1))
            vev = p.inc("s_ve", 1)
            p.wait("scalar", "s_ve", vev)
            p.emit("scalar", lambda s: s.activation(rstd, var, AF.Sqrt)
                   .then_inc(p.sems["s_sc"], 1))
            scv = p.inc("s_sc", 1)
            p.wait("vector", "s_sc", scv)
            p.emit("vector", lambda v: v.reciprocal(rstd, rstd))
            p.emit("vector", lambda v: v.scalar_tensor_tensor(
                nmr, mean, -1.0, rstd, ALU.mult, ALU.mult).then_inc(p.sems["s_ve"], 1))
            vev = p.inc("s_ve", 1)
            p.wait("tensor", "s_ve", vev)
            wait_bank(4)
            wait_bank(5)
            p.emit("tensor", lambda t: t.matmul(P[4][:], ones_r, rstd,
                                                start=True, stop=True))
            p.emit("tensor", lambda t: t.matmul(P[5][:], ones_r, nmr,
                                                start=True, stop=True)
                   .then_inc(p.sems["s_mm"], 1))
            mmv = p.inc("s_mm", 1)
            p.wait("vector", "s_mm", mmv)
            p.emit("vector", lambda v: v.tensor_copy(rrep, P[4][:]))
            p.emit("vector", lambda v: v.tensor_copy(rrep2, P[5][:])
                   .then_inc(p.sems["s_ve"], 1))
            rv = p.inc("s_ve", 1)
            bank_free[4] = ("s_ve", rv)
            bank_free[5] = ("s_ve", rv)
            p.wait("vector", "s_ve", rv)
            for i in range(DC):
                p.emit("vector", lambda v, z=zts[i]:
                       v.tensor_mul(f32a, z, rrep))
                p.emit("vector", lambda v, i=i, tt=tt:
                       v.tensor_add(xo_ap[:, i, tt * width:(tt + 1) * width],
                                    f32a, rrep2))
            p.emit("vector", lambda v: v.memset(f32b[0:1, 0:1], 0.0)
                   .then_inc(p.sems["s_ve"], 1))
            p.inc("s_ve", 1)

    layernorm_fm(qf, x1, 512, False, z_ve)
    ln1_ve = p.cur("s_ve")
    p.wait("sync", "s_ve", ln1_ve)
    p.emit("sync", lambda s: s.dma_start(out=idf, in_=id_in)
           .then_inc(p.sems["s_in"], 16))
    IDF_IN = p.inc("s_in", 16)

    p.wait("tensor", "s_ve", ln1_ve)
    for tt in range(TT):
        wait_bank("PS1")
        for i in range(DC):
            last = i == DC - 1
            p.emit("tensor", lambda t, i=i, tt=tt, last=last:
                   (t.matmul(PS1[:, :], ones_c, x1[:, i, tt * 512:(tt + 1) * 512],
                             start=(i == 0), stop=last).then_inc(p.sems["s_mm"], 1))
                   if last else
                   t.matmul(PS1[:, :], ones_c, x1[:, i, tt * 512:(tt + 1) * 512],
                            start=(i == 0), stop=last))
        mmv = p.inc("s_mm", 1)
        p.wait("vector", "s_mm", mmv)
        p.emit("vector", lambda v, tt=tt: v.tensor_scalar(
            csm[0:1, tt * 512:(tt + 1) * 512], PS1[:, :], -128.0, None,
            ALU.mult).then_inc(p.sems["s_ve"], 1))
        vev = p.inc("s_ve", 1)
        bank_free["PS1"] = ("s_ve", vev)
    csm_ve = p.cur("s_ve")
    tap("t_x1", x1[:].rearrange("p a b -> p (a b)"), "s_ve")

    p.wait("tensor", "s_ve", csm_ve)
    gl_mm = [wu_mm[0], wu_mm[0]]
    ul_mm = [wu_mm[1], wu_mm[1]]
    prev_silu = [0]
    for tt in range(TT):
        csm_t = csm[0:1, tt * 512:(tt + 1) * 512]
        for q_ in range(2):
            gl_mm[q_] = max(gl_mm[q_], wu_mm[0])
            ul_mm[q_] = max(ul_mm[q_], wu_mm[1])
        for ws in range(22):
            par = ws % 2
            p.wait("gpsimd", "s_mm", gl_mm[par])
            p.emit("gpsimd", lambda g, ws=ws, par=par: g.dma_start(
                out=wgb[par][:].rearrange("p a b -> p (a b)"),
                in_=wgT[ws]).then_inc(p.sems[f"s_wgl{par}"], 16))
            p.inc(f"s_wgl{par}", 16)
            p.wait("gpsimd", "s_mm", ul_mm[par])
            p.emit("gpsimd", lambda g, ws=ws, par=par: g.dma_start(
                out=wub[par][:].rearrange("p a b -> p (a b)"),
                in_=wuT[ws]).then_inc(p.sems[f"s_wul{par}"], 16))
            p.inc(f"s_wul{par}", 16)
            p.wait("tensor", f"s_wgl{par}", p.cur(f"s_wgl{par}"))
            p.wait("tensor", f"s_wul{par}", p.cur(f"s_wul{par}"))
            for hc2 in range(2):
                hcg = ws * 2 + hc2
                bank_g = (2, 4)[hc2 % 2]
                bank_u = (3, 5)[hc2 % 2]
                for which, bank, wsb in (("g", bank_g, wgb[par]), ("u", bank_u, wub[par])):
                    mms = [(lambda t, st, sp, i=i, hc2=hc2, tt=tt, bank=bank, wsb=wsb:
                            t.matmul(P[bank][:],
                                     wsb[:, i, hc2 * 128:(hc2 + 1) * 128],
                                     x1[:, i, tt * 512:(tt + 1) * 512],
                                     start=st, stop=False)) for i in range(DC)]
                    mms.append(lambda t, st, sp, bank=bank, csm_t=csm_t:
                               t.matmul(P[bank][:], ones_r, csm_t,
                                        start=False, stop=True))
                    scl = sgf if which == "g" else suf
                    dst = f32a if which == "g" else f32b
                    if which == "g":
                        p.wait("vector", "s_sc", prev_silu[0])
                    def evac(e, bank=bank, hcg=hcg, scl=scl, dst=dst):
                        return e.tensor_scalar(dst, P[bank][:],
                                               scl[:, hcg:hcg + 1], None, ALU.mult)
                    mm_group(bank, P[bank], mms, "vector", evac)
                gl_mm[par] = p.cur("s_mm")
                ul_mm[par] = p.cur("s_mm")
                uv = p.cur("s_ve")
                p.wait("scalar", "s_ve", uv)
                p.emit("scalar", lambda s: s.activation(f32a, f32a, AF.Silu)
                       .then_inc(p.sems["s_sc"], 1))
                scv = p.inc("s_sc", 1)
                prev_silu[0] = scv
                p.wait("vector", "s_sc", scv)
                p.emit("vector", lambda v, hcg=hcg: v.tensor_mul(
                    harena[:, hcg, :], f32a, f32b).then_inc(p.sems["s_ve"], 1))
                p.inc("s_ve", 1)
        h_ve = p.cur("s_ve")

        if tt == 0:
            tap("t_h", harena[:].rearrange("p a b -> p (a b)"), "s_ve")
        p.wait("tensor", "s_ve", h_ve)
        wait_bank("PS1")
        for i in range(HC):
            last = i == HC - 1
            p.emit("tensor", lambda t, i=i, last=last:
                   (t.matmul(PS1[:, :], ones_c, harena[:, i, :],
                             start=(i == 0), stop=last).then_inc(p.sems["s_mm"], 1))
                   if last else
                   t.matmul(PS1[:, :], ones_c, harena[:, i, :],
                            start=(i == 0), stop=last))
        mmv = p.inc("s_mm", 1)
        p.wait("vector", "s_mm", mmv)
        p.emit("vector", lambda v: v.tensor_scalar(
            hsm, PS1[:, :], -128.0, None, ALU.mult).then_inc(p.sems["s_ve"], 1))
        hs_ve = p.inc("s_ve", 1)
        bank_free["PS1"] = ("s_ve", hs_ve)
        p.wait("tensor", "s_ve", hs_ve)

        for ws in range(16):
            par = ws % 2
            wu_mm[par] = max(wu_mm[par],
                             max(gl_mm) if par == 0 else max(ul_mm))
            p.wait("gpsimd", "s_mm", wu_mm[par])
            p.emit("gpsimd", lambda g, ws=ws, par=par: g.dma_start(
                out=wbd[par][:].rearrange("p a b -> p (a b)"),
                in_=wdT[ws]).then_inc(p.sems[f"s_wl{par}"], 16))
            p.inc(f"s_wl{par}", 16)
            use_wslice(par)
            oc = ws
            bank = rot[0] % 2
            rot[0] += 1
            mms = [(lambda t, st, sp, i=i, par=par, bank=bank:
                    t.matmul(P[bank][:], wbd[par][:, i, :], harena[:, i, :],
                             start=st, stop=False)) for i in range(HC)]
            mms.append(lambda t, st, sp, bank=bank:
                       t.matmul(P[bank][:], ones_r, hsm, start=False, stop=True))
            def evac(e, bank=bank, oc=oc):
                return e.tensor_scalar(f32b, P[bank][:], sdf[:, oc:oc + 1],
                                       None, ALU.mult)
            mmv, vev = mm_group(bank, P[bank], mms, "vector", evac)
            p.emit("vector", lambda v, oc=oc, tt=tt: v.tensor_add(
                z2[:, oc, :], f32b,
                x1[:, oc, tt * 512:(tt + 1) * 512]).then_inc(p.sems["s_ve"], 1))
            p.inc("s_ve", 1)
            done_wslice(par)
        z2_ve = p.cur("s_ve")
        if tt == 0:
            tap("t_dn", z2[:].rearrange("p a b -> p (a b)"), "s_ve")

        layernorm_fm(z2, z2, 512, True, z2_ve)
        x2_ve = p.cur("s_ve")

        if tt == 0:
            tap("t_x2", z2[:].rearrange("p a b -> p (a b)"), "s_ve")
        p.wait("tensor", "s_ve", x2_ve)
        p.wait("tensor", "s_in", IDF_IN)
        for tc2 in range(4):
            for i in range(DC):
                bank = 4 + (i % 2)
                wait_bank(bank)
                pt = P[bank][:, 0:128]
                p.emit("tensor", lambda t, pt=pt, i=i, tc2=tc2:
                       t.transpose(pt, z2[:, i, tc2 * 128:(tc2 + 1) * 128], idf)
                       .then_inc(p.sems["s_mm"], 1))
                mmv = p.inc("s_mm", 1)
                p.wait("vector", "s_mm", mmv)
                p.emit("vector", lambda v, pt=pt, i=i: v.tensor_copy(
                    otile[:, i * 128:(i + 1) * 128], pt).then_inc(p.sems["s_ve"], 1))
                vev = p.inc("s_ve", 1)
                bank_free[bank] = ("s_ve", vev)
            ot_ve = p.cur("s_ve")
            p.wait("sync", "s_ve", ot_ve)
            row = tt * 512 + tc2 * 128
            p.emit("sync", lambda s, row=row: s.dma_start(
                out=out[row:row + 128, :], in_=otile).then_inc(p.sems["s_out"], 16))
            p.inc("s_out", 16)
            p.wait("vector", "s_out", p.cur("s_out"))

    p.wait("sync", "s_out", p.cur("s_out"))

    with nc.Block() as block:
        @block.sync
        def _(sync):
            for fn in p.ops["sync"]:
                fn(sync)

        @block.gpsimd
        def _(g):
            for fn in p.ops["gpsimd"]:
                fn(g)

        @block.tensor
        def _(t):
            for fn in p.ops["tensor"]:
                fn(t)

        @block.vector
        def _(v):
            for fn in p.ops["vector"]:
                fn(v)

        @block.scalar
        def _(s):
            for fn in p.ops["scalar"]:
                fn(s)

    return nc


def _u8_low_byte(a_int32):
    v = a_int32.view(np.uint8).reshape(*a_int32.shape, 4)
    return np.ascontiguousarray(v[..., 0] if np.little_endian else v[..., 3])


_NC_CACHE = {}


def kernel(L0_attn_base, L0_attn_Wq, L0_attn_bq, L0_attn_Wk, L0_attn_bk,
           L0_attn_Wv, L0_attn_bv, L0_attn_Wo, L0_attn_bo,
           L0_ffn_wq_gate, L0_ffn_s_gate, L0_ffn_wq_up, L0_ffn_s_up,
           L0_ffn_wq_down, L0_ffn_s_down):
    bf16 = ml_dtypes.bfloat16
    base = np.asarray(L0_attn_base)
    assert base.dtype == bf16

    def pcol(b, nch):
        return np.ascontiguousarray(
            np.asarray(b, dtype=np.float32).reshape(nch, 128).T)

    shared = {
        "wqT": np.ascontiguousarray(np.asarray(L0_attn_Wq).T),
        "wkT": np.ascontiguousarray(np.asarray(L0_attn_Wk).T),
        "wvT": np.ascontiguousarray(np.asarray(L0_attn_Wv).T),
        "woT": np.ascontiguousarray(np.asarray(L0_attn_Wo).T),
        "bq": pcol(L0_attn_bq, DC),
        "bk": pcol(L0_attn_bk, DC),
        "bo": pcol(L0_attn_bo, DC),
        "bv": np.ascontiguousarray(np.asarray(L0_attn_bv)[None, :]),
        "wgT": np.ascontiguousarray(_u8_low_byte(np.asarray(L0_ffn_wq_gate))
                                    .reshape(22, 256, DC, 128)
                                    .transpose(0, 3, 2, 1).reshape(22, 128, DC * 256)),
        "wuT": np.ascontiguousarray(_u8_low_byte(np.asarray(L0_ffn_wq_up))
                                    .reshape(22, 256, DC, 128)
                                    .transpose(0, 3, 2, 1).reshape(22, 128, DC * 256)),
        "wdT": np.ascontiguousarray(_u8_low_byte(np.asarray(L0_ffn_wq_down))
                                    .reshape(16, 128, HC, 128)
                                    .transpose(0, 3, 2, 1).reshape(16, 128, HC * 128)),
        "sg": pcol(L0_ffn_s_gate, HC),
        "su": pcol(L0_ffn_s_up, HC),
        "sd": pcol(L0_ffn_s_down, DC),
        "idf": np.eye(128, dtype=np.float32),
    }

    in_maps = []
    for c in range(8):
        b, q = c // 4, c % 4
        m = dict(shared)
        m["x0fm"] = np.ascontiguousarray(base[b, q * T:(q + 1) * T, :].T)
        in_maps.append(m)

    nc = build_nc()
    res = run_bass_kernel_spmd(nc, in_maps, core_ids=list(range(8)))

    outv = np.empty((B, S, D), dtype=np.float32)
    for c in range(8):
        b, q = c // 4, c % 4
        outv[b, q * T:(q + 1) * T, :] = np.asarray(res.results[c]["out"])
    return outv


# version 04
# speedup vs baseline: 1.2177x; 1.0099x over previous
import numpy as np
import ml_dtypes

import concourse.bass as bass
import concourse.mybir as mybir
from concourse.bass_utils import run_bass_kernel_spmd

BF16 = mybir.dt.bfloat16
F32 = mybir.dt.float32
U8 = mybir.dt.uint8
AF = mybir.ActivationFunctionType
ALU = mybir.AluOpType

B, S, D = 2, 4096, 2048
T = 1024
H = 2
HID = 5632
DC = D // 128
HC = HID // 128
TT = T // 512
SHIFT = 6.0
SCALE = 1.0 / 32.0
EPS = 1e-5


class Prog:
    ENGINES = ("sync", "gpsimd", "tensor", "vector", "scalar")

    def __init__(self, nc):
        self.nc = nc
        self.ops = {e: [] for e in self.ENGINES}
        self.sems = {}
        self.cnt = {}

    def sem(self, name):
        h = self.nc.semaphore(name).__enter__()
        self.sems[name] = h
        self.cnt[name] = 0

    def inc(self, name, amt=1):
        self.cnt[name] += amt
        return self.cnt[name]

    def cur(self, name):
        return self.cnt[name]

    def emit(self, eng, fn):
        self.ops[eng].append(fn)

    def wait(self, eng, name, val):
        if val <= 0:
            return
        h = self.sems[name]
        self.emit(eng, lambda e, h=h, v=val: e.wait_ge(h, v))


def build_nc(taps=()):
    nc = bass.Bass("TRN2", target_bir_lowering=False, debug=False)

    x0_in = nc.dram_tensor("x0fm", [D, T], BF16, kind="ExternalInput").ap()
    wqT = nc.dram_tensor("wqT", [D, D], BF16, kind="ExternalInput").ap()
    wkT = nc.dram_tensor("wkT", [D, D], BF16, kind="ExternalInput").ap()
    wvT = nc.dram_tensor("wvT", [D, D], BF16, kind="ExternalInput").ap()
    woT = nc.dram_tensor("woT", [D, D], BF16, kind="ExternalInput").ap()
    bq_in = nc.dram_tensor("bq", [128, DC], F32, kind="ExternalInput").ap()
    bk_in = nc.dram_tensor("bk", [128, DC], F32, kind="ExternalInput").ap()
    bo_in = nc.dram_tensor("bo", [128, DC], F32, kind="ExternalInput").ap()
    bv_in = nc.dram_tensor("bv", [1, D], BF16, kind="ExternalInput").ap()
    wgT = nc.dram_tensor("wgT", [22, 128, DC * 256], U8, kind="ExternalInput").ap()
    wuT = nc.dram_tensor("wuT", [22, 128, DC * 256], U8, kind="ExternalInput").ap()
    wdT = nc.dram_tensor("wdT", [16, 128, HC * 128], U8, kind="ExternalInput").ap()
    sg_in = nc.dram_tensor("sg", [128, HC], F32, kind="ExternalInput").ap()
    su_in = nc.dram_tensor("su", [128, HC], F32, kind="ExternalInput").ap()
    sd_in = nc.dram_tensor("sd", [128, DC], F32, kind="ExternalInput").ap()
    id_in = nc.dram_tensor("idf", [128, 128], F32, kind="ExternalInput").ap()

    out = nc.dram_tensor("out", [T, D], F32, kind="ExternalOutput").ap()
    tap_t = {}
    for tn, shape, dt in (("t_k", [128, DC * T], BF16), ("t_v", [128, 8 * D], BF16),
                          ("t_q", [128, DC * T], BF16), ("t_y", [128, DC * T], BF16),
                          ("t_z", [128, DC * T], BF16), ("t_x1", [128, DC * T], BF16),
                          ("t_h", [128, HC * 512], BF16), ("t_dn", [128, DC * 512], F32),
                          ("t_x2", [128, DC * 512], F32),
                          ("t_p", [128, 32 * 512], BF16)):
        if tn in taps:
            tap_t[tn] = nc.dram_tensor(tn, shape, dt, kind="ExternalOutput").ap()

    cc_in_k = nc.dram_tensor("cc_in_k", [D, T // 2], F32, kind="Internal").ap()
    cc_out_k = [nc.dram_tensor(f"cc_out_k{h}", [4 * D // 2, T // 2], F32,
                               kind="Internal").ap() for h in range(2)]
    cc_in_v = [nc.dram_tensor(f"cc_in_v{h}", [T, D // 4], F32,
                              kind="Internal").ap() for h in range(2)]
    cc_out_v = [nc.dram_tensor(f"cc_out_v{h}", [4 * T, D // 4], F32,
                               kind="Internal").ap() for h in range(2)]
    RG = [[0, 1, 2, 3], [4, 5, 6, 7]]

    AVAIL = nc.sbuf_bytes_remaining
    NB = AVAIL // 256 * 256
    arena = nc.alloc_sbuf_tensor("arena", [128, NB], U8)

    def carve(off, shape, dtype, nparts=128):
        es = mybir.dt.size(dtype)
        n = es
        for s in shape[1:]:
            n *= s
        assert off % 64 == 0 and off + n <= NB, (off, n, NB)
        ap = arena[0:nparts, off:off + n].bitcast(dtype)
        if len(shape) == 2:
            return ap
        if len(shape) == 3:
            return ap.rearrange("p (a b) -> p a b", a=shape[1])
        raise AssertionError(shape)

    KB = 1024
    x0 = carve(0, [128, DC, T], BF16)
    harena = carve(0, [128, HC, 512], BF16)
    zarena = carve(32 * KB, [128, DC, T], BF16)
    qf = zarena
    otile = carve(44 * KB, [128, D], F32)
    csm = carve(52 * KB, [1, T], F32, 1)
    hsm = carve(56 * KB, [1, 512], F32, 1)
    x1 = carve(64 * KB, [128, DC, T], BF16)
    kown = x1
    vown = carve(96 * KB, [128, 8, D], BF16)
    pview = carve(96 * KB, [128, 32, 512], BF16)
    zsq = carve(96 * KB, [128, DC, 512], BF16)
    f32a = carve(112 * KB, [128, 512], F32)
    f32b = carve(114 * KB, [128, 512], F32)
    atile = [carve(116 * KB, [128, 512], BF16), carve(117 * KB, [128, 512], BF16)]
    stat = carve(118 * KB, [1, 5 * 512], F32, 1)
    idf = carve(58 * KB, [128, 128], F32)
    yf = carve(128 * KB, [128, DC, T], BF16)
    z2 = carve(128 * KB, [128, DC, 512], F32)
    bvb = carve(128 * KB, [1, D], BF16, 1)
    wb = [carve(160 * KB, [128, DC, 512], BF16),
          carve(176 * KB, [128, DC, 512], BF16)]
    wgb = [carve(160 * KB, [128, DC, 256], BF16), carve(168 * KB, [128, DC, 256], BF16)]
    wub = [carve(176 * KB, [128, DC, 256], BF16), carve(184 * KB, [128, DC, 256], BF16)]
    wbd = [carve(160 * KB, [128, HC, 128], BF16),
           carve(176 * KB, [128, HC, 128], BF16)]
    kb = [carve(192 * KB, [128, 8, 128], BF16), carve(194 * KB, [128, 8, 128], BF16),
          carve(64 * KB, [128, 8, 128], BF16), carve(66 * KB, [128, 8, 128], BF16)]
    vb = [carve(196 * KB, [128, 8, 128], BF16), carve(198 * KB, [128, 8, 128], BF16),
          carve(68 * KB, [128, 8, 128], BF16), carve(70 * KB, [128, 8, 128], BF16)]
    recd = carve(200 * KB, [1, 512], F32, 1)
    rrep = carve(202 * KB, [128, 512], F32)
    rrep2 = carve(204 * KB, [128, 512], F32)
    sm = 206 * KB
    bqf = carve(sm, [128, DC], F32); sm += 64
    bkf = carve(sm, [128, DC], F32); sm += 64
    bof = carve(sm, [128, DC], F32); sm += 64
    sdf = carve(sm, [128, DC], F32); sm += 64
    sgf = carve(sm, [128, HC], F32); sm += 192
    suf = carve(sm, [128, HC], F32); sm += 192
    ones_c = carve(sm, [128, 1], BF16); sm += 64
    ones_cf = carve(sm, [128, 1], F32); sm += 64
    ones_rb = carve(sm, [1, 128], BF16, 1); sm += 256
    ones_r = carve(sm, [1, 128], F32, 1); sm += 512
    nshift = carve(sm, [128, 1], F32); sm += 64
    assert sm <= NB

    P = [nc.alloc_psum_tensor(f"P{i}", [128, 512], F32) for i in range(6)]
    PS1 = nc.alloc_psum_tensor("PS1", [1, 512], F32)
    PS2 = nc.alloc_psum_tensor("PS2", [1, 512], F32)

    p = Prog(nc)
    for s in ("s_in", "s_ccin", "s_ag", "s_mm", "s_sc", "s_ve", "s_out",
              "s_wl0", "s_wl1", "s_wu0", "s_wu1",
              "s_kbl0", "s_kbl1", "s_kbl2", "s_kbl3",
              "s_vbl0", "s_vbl1", "s_vbl2", "s_vbl3",
              "s_wgl0", "s_wgl1", "s_wul0", "s_wul1"):
        p.sem(s)

    def tap(tn, src_ap, sem_name):
        if tn not in tap_t:
            return
        p.wait("sync", sem_name, p.cur(sem_name))
        p.emit("sync", lambda s, tn=tn, a=src_ap: s.dma_start(
            out=tap_t[tn], in_=a).then_inc(p.sems["s_out"], 16))
        p.inc("s_out", 16)

    bank_free = {i: ("s_sc", 0) for i in list(range(6)) + ["PS1", "PS2"]}

    def wait_bank(bank, eng="tensor"):
        sname, val = bank_free[bank]
        p.wait(eng, sname, val)

    wl_cnt = [0, 0]
    wu_mm = [0, 0]

    def load_wslice(src_ap, ws, par, u8=False, down=False):
        eng = "gpsimd" if u8 else "sync"
        p.wait(eng, "s_mm", wu_mm[par])
        if down:
            dst = wbd[par][:]
            src = src_ap.rearrange("(c p) o -> p c o", p=128)[:, :, ws * 128:(ws + 1) * 128]
        else:
            dst = wb[par][:]
            src = src_ap.rearrange("(c p) o -> p c o", p=128)[:, :, ws * 512:(ws + 1) * 512]
        p.emit(eng, lambda e, dst=dst, src=src, par=par: e.dma_start(
            out=dst, in_=src).then_inc(p.sems[f"s_wl{par}"], 16))
        p.inc(f"s_wl{par}", 16)

    def use_wslice(par):
        p.wait("tensor", f"s_wl{par}", p.cur(f"s_wl{par}"))

    def done_wslice(par, last_mm_inst=None):
        wu_mm[par] = p.cur("s_mm")

    def mm_group(bank, psum_ap, mms, evac_eng, evac_fn):
        wait_bank(bank)
        n = len(mms)
        for j, emitfn in enumerate(mms):
            inc = (j == n - 1)
            p.emit("tensor", lambda t, f=emitfn, j=j, inc=inc:
                   (f(t, j == 0, j == n - 1).then_inc(p.sems["s_mm"], 1)
                    if inc else f(t, j == 0, j == n - 1)))
        mmv = p.inc("s_mm", 1)
        p.wait(evac_eng, "s_mm", mmv)
        sem = "s_sc" if evac_eng == "scalar" else "s_ve"
        p.emit(evac_eng, lambda e, f=evac_fn, sem=sem: f(e).then_inc(p.sems[sem], 1))
        ev = p.inc(sem, 1)
        bank_free[bank] = (sem, ev)
        return mmv, ev

    loads_sync = [(bvb, bv_in), (sgf, sg_in), (suf, su_in), (sdf, sd_in),
                  (bqf, bq_in), (bkf, bk_in), (bof, bo_in)]
    for t, src in loads_sync:
        p.emit("sync", lambda s, t=t, src=src: s.dma_start(
            out=t, in_=src).then_inc(p.sems["s_in"], 16))
        p.inc("s_in", 16)
    p.emit("sync", lambda s: s.dma_start(
        out=x0, in_=x0_in.rearrange("(c p) t -> p c t", p=128)
    ).then_inc(p.sems["s_in"], 16))
    p.inc("s_in", 16)
    SETUP_IN = p.cur("s_in")
    p.emit("vector", lambda v: v.memset(ones_c, 1.0))
    p.emit("vector", lambda v: v.memset(ones_cf, 1.0))
    p.emit("vector", lambda v: v.memset(ones_rb, 1.0))
    p.emit("vector", lambda v: v.memset(ones_r, 1.0))
    p.emit("vector", lambda v: v.memset(nshift, -SHIFT).then_inc(p.sems["s_ve"], 1))
    CONST_VE = p.inc("s_ve", 1)

    p.wait("tensor", "s_in", SETUP_IN)
    p.wait("tensor", "s_ve", CONST_VE)
    p.wait("scalar", "s_in", SETUP_IN)

    rot = [0]

    def proj_wstat(wsrc, bias, dst, ws_range=range(4)):
        for ws in ws_range:
            par = ws % 2
            load_wslice(wsrc, ws, par)
            use_wslice(par)
            for tt in range(TT):
                for oc2 in range(4):
                    oc = ws * 4 + oc2
                    bank = rot[0] % 2
                    rot[0] += 1
                    mms = [
                        (lambda t, st, sp, par=par, i=i, oc2=oc2, tt=tt, bank=bank:
                         t.matmul(P[bank][:],
                                  wb[par][:, i, oc2 * 128:(oc2 + 1) * 128],
                                  qf[:, i, tt * 512:(tt + 1) * 512] if False else
                                  x0[:, i, tt * 512:(tt + 1) * 512],
                                  start=st, stop=sp))
                        for i in range(DC)]
                    def evac(e, bank=bank, oc=oc, tt=tt, dst=dst, bias=bias):
                        return e.activation(dst[:, oc, tt * 512:(tt + 1) * 512],
                                            P[bank][:], AF.Relu,
                                            bias=bias[:, oc:oc + 1])
                    mm_group(bank, P[bank], mms, "scalar", evac)
            done_wslice(par)

    def proj_v(ws_range):
        for ws in ws_range:
            par = ws % 2
            load_wslice(wvT, ws, par)
            use_wslice(par)
            for tc in range(8):
                bank = rot[0] % 2
                rot[0] += 1
                mms = [(lambda t, st, sp, par=par, i=i, tc=tc, bank=bank:
                        t.matmul(P[bank][:], x0[:, i, tc * 128:(tc + 1) * 128],
                                 wb[par][:, i, :], start=st, stop=False))
                       for i in range(DC)]
                mms.append(lambda t, st, sp, ws=ws, bank=bank:
                           t.matmul(P[bank][:], ones_rb,
                                    bvb[0:1, ws * 512:(ws + 1) * 512],
                                    start=False, stop=True))
                def evac(e, bank=bank, tc=tc, ws=ws):
                    return e.activation(vown[:, tc, ws * 512:(ws + 1) * 512],
                                        P[bank][:], AF.Relu)
                mm_group(bank, P[bank], mms, "scalar", evac)
            done_wslice(par)

    def kick_k(h):
        sc = p.cur("s_sc")
        p.wait("gpsimd", "s_sc", sc)
        p.emit("gpsimd", lambda g, h=h: g.dma_start(
            out=cc_in_k.rearrange("(c p) t -> p c t", p=128)[:, 8 * h:8 * h + 8, :],
            in_=kown[:, 8 * h:8 * h + 8, :].rearrange("p a b -> p (a b)").bitcast(F32)
                .rearrange("p (a b) -> p a b", a=8)).then_inc(p.sems["s_ccin"], 16))
        p.inc("s_ccin", 16)
        p.wait("gpsimd", "s_ccin", p.cur("s_ccin"))
        p.emit("gpsimd", lambda g, h=h: g.collective_compute(
            "AllGather", ALU.bypass, replica_groups=RG,
            ins=[cc_in_k[1024 * h:1024 * (h + 1), :]],
            outs=[cc_out_k[h][:, :]]).then_inc(p.sems["s_ag"], 1))
        p.inc("s_ag", 1)

    def kick_v(h):
        sc = p.cur("s_sc")
        p.wait("gpsimd", "s_sc", sc)
        vflat = vown[:].rearrange("p a b -> p (a b)").bitcast(F32) \
            .rearrange("p (a b) -> p a b", a=8)
        p.emit("gpsimd", lambda g, h=h: g.dma_start(
            out=cc_in_v[h].rearrange("(c p) t -> p c t", p=128),
            in_=vflat[:, :, h * 512:(h + 1) * 512]).then_inc(p.sems["s_ccin"], 16))
        p.inc("s_ccin", 16)
        p.wait("gpsimd", "s_ccin", p.cur("s_ccin"))
        p.emit("gpsimd", lambda g, h=h: g.collective_compute(
            "AllGather", ALU.bypass, replica_groups=RG,
            ins=[cc_in_v[h][:, :]], outs=[cc_out_v[h][:, :]]).then_inc(p.sems["s_ag"], 1))
        p.inc("s_ag", 1)

    proj_wstat(wkT, bkf, kown, range(0, 2))
    kick_k(0)
    proj_v(range(0, 2))
    kick_v(0)
    proj_wstat(wkT, bkf, kown, range(2, 4))
    kick_k(1)
    proj_v(range(2, 4))
    kick_v(1)
    proj_wstat(wqT, bqf, qf)

    ccok = [cc_out_k[h].rearrange("(n p) c -> n p c", p=128) for h in range(2)]
    ccov = [cc_out_v[h].rearrange("(n p) c -> n p c", p=128) for h in range(2)]

    kb_mm = [0, 0, 0, 0]
    vb_mm = [0, 0, 0, 0]

    def load_kb(h, kc, par):
        p.wait("sync", "s_mm", kb_mm[par])
        r, j = kc // 8, kc % 8
        src = ccok[h][r * 8: r * 8 + 8, :, j * 64:(j + 1) * 64]
        p.emit("sync", lambda s, src=src, par=par: s.dma_start(
            out=kb[par][:].rearrange("p a b -> p (a b)").bitcast(F32)
                .rearrange("p (a b) -> p a b", a=8),
            in_=src.rearrange("a p b -> p a b")).then_inc(p.sems[f"s_kbl{par}"], 16))
        p.inc(f"s_kbl{par}", 16)

    def load_vb(h, kc, par):
        p.wait("sync", "s_mm", vb_mm[par])
        src = ccov[h][kc, :, :]
        p.emit("sync", lambda s, src=src, par=par: s.dma_start(
            out=vb[par][:].rearrange("p a b -> p (a b)").bitcast(F32),
            in_=src).then_inc(p.sems[f"s_vbl{par}"], 16))
        p.inc(f"s_vbl{par}", 16)

    p.wait("scalar", "s_ccin", p.cur("s_ccin"))

    for h in range(H):
        for qt in range(TT):
            if qt == 0:
                p.wait("sync", "s_ag", 2 * h + 1)
                p.wait("sync", "s_ag", 2 * h + 2)
            exp_sc = {}
            for hc in range(4):
                wait_bank(hc)
            wait_bank("PS1")
            for kc in range(34):
                if kc < 32:
                    par = kc % 4
                    load_kb(h, kc, par)
                    load_vb(h, kc, par)
                    sb_bank = 4 + (kc % 2)
                    wait_bank(sb_bank)
                    p.wait("tensor", f"s_kbl{par}", p.cur(f"s_kbl{par}"))
                    for hc in range(8):
                        last = hc == 7
                        p.emit("tensor", lambda t, b=sb_bank, par=par, hc=hc, h=h,
                               qt=qt, last=last:
                               (t.matmul(P[b][:], kb[par][:, hc, :],
                                         qf[:, 8 * h + hc, qt * 512:(qt + 1) * 512],
                                         start=(hc == 0), stop=last)
                                .then_inc(p.sems["s_mm"], 1)) if last else
                               t.matmul(P[b][:], kb[par][:, hc, :],
                                        qf[:, 8 * h + hc, qt * 512:(qt + 1) * 512],
                                        start=(hc == 0), stop=last))
                    mmv = p.inc("s_mm", 1)
                    kb_mm[par] = mmv
                    p.wait("scalar", "s_mm", mmv)
                    p.emit("scalar", lambda s, b=sb_bank, kc=kc:
                           s.activation(pview[:, kc, :], P[b][:], AF.Exp,
                                        bias=nshift, scale=SCALE)
                           .then_inc(p.sems["s_sc"], 1))
                    scv = p.inc("s_sc", 1)
                    exp_sc[kc] = scv
                    bank_free[sb_bank] = ("s_sc", scv)
                if kc >= 2:
                    kp = kc - 2
                    parp = kp % 4
                    p.wait("tensor", "s_sc", exp_sc[kp])
                    p.wait("tensor", f"s_vbl{parp}", p.cur(f"s_vbl{parp}"))
                    for hc in range(4):
                        p.emit("tensor", lambda t, hc=hc, parp=parp, kp=kp:
                               t.matmul(P[hc][:], vb[parp][:, hc, :], pview[:, kp, :],
                                        start=(kp == 0), stop=(kp == 31)))
                    p.emit("tensor", lambda t, kp=kp:
                           t.matmul(PS1[:, :], ones_c, pview[:, kp, :],
                                    start=(kp == 0), stop=(kp == 31))
                           .then_inc(p.sems["s_mm"], 1))
                    vb_mm[parp] = p.inc("s_mm", 1)
            mmv = p.cur("s_mm")
            p.wait("vector", "s_mm", mmv)
            p.emit("vector", lambda v: v.reciprocal(recd, PS1[:, :])
                   .then_inc(p.sems["s_ve"], 1))
            vev = p.inc("s_ve", 1)
            bank_free["PS1"] = ("s_ve", vev)
            p.wait("tensor", "s_ve", vev)
            wait_bank(4)
            p.emit("tensor", lambda t: t.matmul(P[4][:], ones_r, recd,
                                                start=True, stop=True)
                   .then_inc(p.sems["s_mm"], 1))
            mmv = p.inc("s_mm", 1)
            p.wait("vector", "s_mm", mmv)
            p.emit("vector", lambda v: v.tensor_copy(rrep, P[4][:])
                   .then_inc(p.sems["s_ve"], 1))
            rrv = p.inc("s_ve", 1)
            bank_free[4] = ("s_ve", rrv)
            p.wait("vector", "s_ve", rrv)
            for hc in range(4):
                p.emit("vector", lambda v, hc=hc, h=h, qt=qt:
                       v.tensor_mul(yf[:, 8 * h + hc, qt * 512:(qt + 1) * 512],
                                    P[hc][:], rrep).then_inc(p.sems["s_ve"], 1))
                vev = p.inc("s_ve", 1)
                bank_free[hc] = ("s_ve", vev)
            for kc in range(32):
                par = kc % 4
                load_vb(h, kc, par)
                p.wait("tensor", f"s_vbl{par}", p.cur(f"s_vbl{par}"))
                if kc == 0:
                    for hc in range(4):
                        wait_bank(hc)
                for hc in range(4):
                    p.emit("tensor", lambda t, hc=hc, par=par, kc=kc, last=(hc == 3):
                           (t.matmul(P[hc][:], vb[par][:, 4 + hc, :], pview[:, kc, :],
                                     start=(kc == 0), stop=(kc == 31))
                            .then_inc(p.sems["s_mm"], 1)) if last else
                           t.matmul(P[hc][:], vb[par][:, 4 + hc, :], pview[:, kc, :],
                                    start=(kc == 0), stop=(kc == 31)))
                vb_mm[par] = p.inc("s_mm", 1)
            mmv = p.cur("s_mm")
            p.wait("vector", "s_mm", mmv)
            for hc in range(4):
                p.emit("vector", lambda v, hc=hc, h=h, qt=qt:
                       v.tensor_mul(yf[:, 8 * h + 4 + hc, qt * 512:(qt + 1) * 512],
                                    P[hc][:], rrep).then_inc(p.sems["s_ve"], 1))
                vev = p.inc("s_ve", 1)
                bank_free[hc] = ("s_ve", vev)

    y_ve = p.cur("s_ve")
    tap("t_y", yf[:].rearrange("p a b -> p (a b)"), "s_ve")
    tap("t_p", pview[:].rearrange("p a b -> p (a b)"), "s_ve")

    at_guard = [0, 0]
    p.wait("tensor", "s_ve", y_ve)
    for ws in range(4):
        par = ws % 2
        load_wslice(woT, ws, par)
        use_wslice(par)
        for tt in range(TT):
            for oc2 in range(4):
                oc = ws * 4 + oc2
                bank = rot[0] % 2
                rot[0] += 1
                apar = (tt * 4 + oc2) % 2
                mms = [(lambda t, st, sp, par=par, i=i, oc2=oc2, tt=tt, bank=bank:
                        t.matmul(P[bank][:],
                                 wb[par][:, i, oc2 * 128:(oc2 + 1) * 128],
                                 yf[:, i, tt * 512:(tt + 1) * 512],
                                 start=st, stop=sp)) for i in range(DC)]
                p.wait("scalar", "s_ve", at_guard[apar])
                def evac(e, bank=bank, oc=oc, apar=apar):
                    return e.activation(atile[apar][:], P[bank][:], AF.Relu,
                                        bias=bof[:, oc:oc + 1])
                mmv, scv = mm_group(bank, P[bank], mms, "scalar", evac)
                p.wait("vector", "s_sc", scv)
                p.emit("vector", lambda v, oc=oc, tt=tt, apar=apar:
                       v.tensor_add(qf[:, oc, tt * 512:(tt + 1) * 512],
                                    atile[apar][:],
                                    x0[:, oc, tt * 512:(tt + 1) * 512])
                       .then_inc(p.sems["s_ve"], 1))
                at_guard[apar] = p.inc("s_ve", 1)
        done_wslice(par)
    z_ve = p.cur("s_ve")
    tap("t_z", qf[:].rearrange("p a b -> p (a b)"), "s_ve")

    def layernorm_fm(z_ap, xo_ap, width, f32src, ln_ready_ve, apply=True):
        mean = stat[0:1, 0:512]
        msq = stat[0:1, 512:1024]
        var = stat[0:1, 1024:1536]
        rstd = stat[0:1, 1536:2048]
        nmr = stat[0:1, 2048:2560]
        ntt = z_ap.shape[2] // width
        for tt in range(ntt):
            zts = [z_ap[:, i, tt * width:(tt + 1) * width] for i in range(DC)]
            p.wait("scalar", "s_ve", ln_ready_ve)
            if tt > 0:
                p.wait("scalar", "s_mm", p.cur("s_mm"))
            for i in range(DC):
                p.emit("scalar", lambda s, i=i, z=zts[i]:
                       s.activation(zsq[:, i, :], z, AF.Square)
                       .then_inc(p.sems["s_sc"], 1))
                p.inc("s_sc", 1)
            sq_sc = p.cur("s_sc")
            p.wait("tensor", "s_sc", sq_sc)
            wait_bank("PS1")
            wait_bank("PS2")
            oz = ones_cf if f32src else ones_c
            for i in range(DC):
                last = i == DC - 1
                p.emit("tensor", lambda t, i=i, z=zts[i], oz=oz, last=last:
                       (t.matmul(PS1[:, :], oz, z, start=(i == 0), stop=last)
                        .then_inc(p.sems["s_mm"], 1)) if last else
                       t.matmul(PS1[:, :], oz, z, start=(i == 0), stop=last))
            p.inc("s_mm", 1)
            for i in range(DC):
                last = i == DC - 1
                p.emit("tensor", lambda t, i=i, last=last:
                       (t.matmul(PS2[:, :], ones_c, zsq[:, i, :],
                                 start=(i == 0), stop=last)
                        .then_inc(p.sems["s_mm"], 1)) if last else
                       t.matmul(PS2[:, :], ones_c, zsq[:, i, :],
                                start=(i == 0), stop=last))
            mmv = p.inc("s_mm", 1)
            p.wait("vector", "s_mm", mmv)
            p.emit("vector", lambda v: v.tensor_scalar(mean, PS1[:, :], 1.0 / D,
                                                       None, ALU.mult))
            p.emit("vector", lambda v: v.tensor_scalar(msq, PS2[:, :], 1.0 / D,
                                                       None, ALU.mult)
                   .then_inc(p.sems["s_ve"], 1))
            vev = p.inc("s_ve", 1)
            bank_free["PS1"] = ("s_ve", vev)
            bank_free["PS2"] = ("s_ve", vev)
            p.emit("vector", lambda v: v.scalar_tensor_tensor(
                var, mean, -1.0, mean, ALU.mult, ALU.mult))
            p.emit("vector", lambda v: v.tensor_add(var, var, msq))
            p.emit("vector", lambda v: v.tensor_scalar(
                var, var, 1.0, EPS, ALU.mult, ALU.add).then_inc(p.sems["s_ve"], 1))
            vev = p.inc("s_ve", 1)
            p.wait("scalar", "s_ve", vev)
            p.emit("scalar", lambda s: s.activation(rstd, var, AF.Sqrt)
                   .then_inc(p.sems["s_sc"], 1))
            scv = p.inc("s_sc", 1)
            p.wait("vector", "s_sc", scv)
            p.emit("vector", lambda v: v.reciprocal(rstd, rstd))
            p.emit("vector", lambda v: v.scalar_tensor_tensor(
                nmr, mean, -1.0, rstd, ALU.mult, ALU.mult).then_inc(p.sems["s_ve"], 1))
            vev = p.inc("s_ve", 1)
            if not apply:
                continue
            p.wait("tensor", "s_ve", vev)
            wait_bank(4)
            wait_bank(5)
            p.emit("tensor", lambda t: t.matmul(P[4][:], ones_r, rstd,
                                                start=True, stop=True))
            p.emit("tensor", lambda t: t.matmul(P[5][:], ones_r, nmr,
                                                start=True, stop=True)
                   .then_inc(p.sems["s_mm"], 1))
            mmv = p.inc("s_mm", 1)
            p.wait("vector", "s_mm", mmv)
            p.emit("vector", lambda v: v.tensor_copy(rrep, P[4][:]))
            p.emit("vector", lambda v: v.tensor_copy(rrep2, P[5][:])
                   .then_inc(p.sems["s_ve"], 1))
            rv = p.inc("s_ve", 1)
            bank_free[4] = ("s_ve", rv)
            bank_free[5] = ("s_ve", rv)
            p.wait("vector", "s_ve", rv)
            for i in range(DC):
                p.emit("vector", lambda v, z=zts[i]:
                       v.tensor_mul(f32a, z, rrep))
                p.emit("vector", lambda v, i=i, tt=tt:
                       v.tensor_add(xo_ap[:, i, tt * width:(tt + 1) * width],
                                    f32a, rrep2))
            p.emit("vector", lambda v: v.memset(f32b[0:1, 0:1], 0.0)
                   .then_inc(p.sems["s_ve"], 1))
            p.inc("s_ve", 1)

    layernorm_fm(qf, x1, 512, False, z_ve)
    ln1_ve = p.cur("s_ve")
    p.wait("sync", "s_ve", ln1_ve)
    p.emit("sync", lambda s: s.dma_start(out=idf, in_=id_in)
           .then_inc(p.sems["s_in"], 16))
    IDF_IN = p.inc("s_in", 16)

    p.wait("tensor", "s_ve", ln1_ve)
    for tt in range(TT):
        wait_bank("PS1")
        for i in range(DC):
            last = i == DC - 1
            p.emit("tensor", lambda t, i=i, tt=tt, last=last:
                   (t.matmul(PS1[:, :], ones_c, x1[:, i, tt * 512:(tt + 1) * 512],
                             start=(i == 0), stop=last).then_inc(p.sems["s_mm"], 1))
                   if last else
                   t.matmul(PS1[:, :], ones_c, x1[:, i, tt * 512:(tt + 1) * 512],
                            start=(i == 0), stop=last))
        mmv = p.inc("s_mm", 1)
        p.wait("vector", "s_mm", mmv)
        p.emit("vector", lambda v, tt=tt: v.tensor_scalar(
            csm[0:1, tt * 512:(tt + 1) * 512], PS1[:, :], -128.0, None,
            ALU.mult).then_inc(p.sems["s_ve"], 1))
        vev = p.inc("s_ve", 1)
        bank_free["PS1"] = ("s_ve", vev)
    csm_ve = p.cur("s_ve")
    tap("t_x1", x1[:].rearrange("p a b -> p (a b)"), "s_ve")

    p.wait("tensor", "s_ve", csm_ve)
    gl_mm = [wu_mm[0], wu_mm[0]]
    ul_mm = [wu_mm[1], wu_mm[1]]
    prev_silu = [0]
    for tt in range(TT):
        csm_t = csm[0:1, tt * 512:(tt + 1) * 512]
        for q_ in range(2):
            gl_mm[q_] = max(gl_mm[q_], wu_mm[0])
            ul_mm[q_] = max(ul_mm[q_], wu_mm[1])
        for ws in range(22):
            par = ws % 2
            p.wait("gpsimd", "s_mm", gl_mm[par])
            p.emit("gpsimd", lambda g, ws=ws, par=par: g.dma_start(
                out=wgb[par][:].rearrange("p a b -> p (a b)"),
                in_=wgT[ws]).then_inc(p.sems[f"s_wgl{par}"], 16))
            p.inc(f"s_wgl{par}", 16)
            p.wait("gpsimd", "s_mm", ul_mm[par])
            p.emit("gpsimd", lambda g, ws=ws, par=par: g.dma_start(
                out=wub[par][:].rearrange("p a b -> p (a b)"),
                in_=wuT[ws]).then_inc(p.sems[f"s_wul{par}"], 16))
            p.inc(f"s_wul{par}", 16)
            p.wait("tensor", f"s_wgl{par}", p.cur(f"s_wgl{par}"))
            p.wait("tensor", f"s_wul{par}", p.cur(f"s_wul{par}"))
            for hc2 in range(2):
                hcg = ws * 2 + hc2
                bank_g = (2, 4)[hc2 % 2]
                bank_u = (3, 5)[hc2 % 2]
                for which, bank, wsb in (("g", bank_g, wgb[par]), ("u", bank_u, wub[par])):
                    mms = [(lambda t, st, sp, i=i, hc2=hc2, tt=tt, bank=bank, wsb=wsb:
                            t.matmul(P[bank][:],
                                     wsb[:, i, hc2 * 128:(hc2 + 1) * 128],
                                     x1[:, i, tt * 512:(tt + 1) * 512],
                                     start=st, stop=False)) for i in range(DC)]
                    mms.append(lambda t, st, sp, bank=bank, csm_t=csm_t:
                               t.matmul(P[bank][:], ones_r, csm_t,
                                        start=False, stop=True))
                    scl = sgf if which == "g" else suf
                    dst = f32a if which == "g" else f32b
                    if which == "g":
                        p.wait("vector", "s_sc", prev_silu[0])
                    def evac(e, bank=bank, hcg=hcg, scl=scl, dst=dst):
                        return e.tensor_scalar(dst, P[bank][:],
                                               scl[:, hcg:hcg + 1], None, ALU.mult)
                    mm_group(bank, P[bank], mms, "vector", evac)
                gl_mm[par] = p.cur("s_mm")
                ul_mm[par] = p.cur("s_mm")
                uv = p.cur("s_ve")
                p.wait("scalar", "s_ve", uv)
                p.emit("scalar", lambda s: s.activation(f32a, f32a, AF.Silu)
                       .then_inc(p.sems["s_sc"], 1))
                scv = p.inc("s_sc", 1)
                prev_silu[0] = scv
                p.wait("vector", "s_sc", scv)
                p.emit("vector", lambda v, hcg=hcg: v.tensor_mul(
                    harena[:, hcg, :], f32a, f32b).then_inc(p.sems["s_ve"], 1))
                p.inc("s_ve", 1)
        h_ve = p.cur("s_ve")

        if tt == 0:
            tap("t_h", harena[:].rearrange("p a b -> p (a b)"), "s_ve")
        p.wait("tensor", "s_ve", h_ve)
        wait_bank("PS1")
        for i in range(HC):
            last = i == HC - 1
            p.emit("tensor", lambda t, i=i, last=last:
                   (t.matmul(PS1[:, :], ones_c, harena[:, i, :],
                             start=(i == 0), stop=last).then_inc(p.sems["s_mm"], 1))
                   if last else
                   t.matmul(PS1[:, :], ones_c, harena[:, i, :],
                            start=(i == 0), stop=last))
        mmv = p.inc("s_mm", 1)
        p.wait("vector", "s_mm", mmv)
        p.emit("vector", lambda v: v.tensor_scalar(
            hsm, PS1[:, :], -128.0, None, ALU.mult).then_inc(p.sems["s_ve"], 1))
        hs_ve = p.inc("s_ve", 1)
        bank_free["PS1"] = ("s_ve", hs_ve)
        p.wait("tensor", "s_ve", hs_ve)

        for ws in range(16):
            par = ws % 2
            wu_mm[par] = max(wu_mm[par],
                             max(gl_mm) if par == 0 else max(ul_mm))
            p.wait("gpsimd", "s_mm", wu_mm[par])
            p.emit("gpsimd", lambda g, ws=ws, par=par: g.dma_start(
                out=wbd[par][:].rearrange("p a b -> p (a b)"),
                in_=wdT[ws]).then_inc(p.sems[f"s_wl{par}"], 16))
            p.inc(f"s_wl{par}", 16)
            use_wslice(par)
            oc = ws
            bank = rot[0] % 2
            rot[0] += 1
            mms = [(lambda t, st, sp, i=i, par=par, bank=bank:
                    t.matmul(P[bank][:], wbd[par][:, i, :], harena[:, i, :],
                             start=st, stop=False)) for i in range(HC)]
            mms.append(lambda t, st, sp, bank=bank:
                       t.matmul(P[bank][:], ones_r, hsm, start=False, stop=True))
            def evac(e, bank=bank, oc=oc, tt=tt):
                return e.scalar_tensor_tensor(
                    z2[:, oc, :], P[bank][:], sdf[:, oc:oc + 1],
                    x1[:, oc, tt * 512:(tt + 1) * 512], ALU.mult, ALU.add)
            mm_group(bank, P[bank], mms, "vector", evac)
            done_wslice(par)
        z2_ve = p.cur("s_ve")
        if tt == 0:
            tap("t_dn", z2[:].rearrange("p a b -> p (a b)"), "s_ve")

        layernorm_fm(z2, z2, 512, True, z2_ve, apply=False)
        rstd_s = stat[0:1, 1536:2048]
        nmr_s = stat[0:1, 2048:2560]
        x2_ve = p.cur("s_ve")

        p.wait("tensor", "s_ve", x2_ve)
        p.wait("tensor", "s_in", IDF_IN)
        for tc2 in range(4):
            wait_bank(4)
            wait_bank(5)
            p.emit("tensor", lambda t, tc2=tc2: t.matmul(
                P[4][:, 0:1], rstd_s[0:1, tc2 * 128:(tc2 + 1) * 128],
                ones_r[0:1, 0:1], start=True, stop=True))
            p.emit("tensor", lambda t, tc2=tc2: t.matmul(
                P[5][:, 0:1], nmr_s[0:1, tc2 * 128:(tc2 + 1) * 128],
                ones_r[0:1, 0:1], start=True, stop=True)
                   .then_inc(p.sems["s_mm"], 1))
            mmv = p.inc("s_mm", 1)
            p.wait("vector", "s_mm", mmv)
            p.emit("vector", lambda v: v.tensor_copy(f32a[:, 0:1], P[4][:, 0:1]))
            p.emit("vector", lambda v: v.tensor_copy(f32a[:, 1:2], P[5][:, 0:1])
                   .then_inc(p.sems["s_ve"], 1))
            sc_ve = p.inc("s_ve", 1)
            bank_free[4] = ("s_ve", sc_ve)
            bank_free[5] = ("s_ve", sc_ve)
            p.wait("tensor", "s_ve", sc_ve)
            for i in range(DC):
                bank = 4 + (i % 2)
                wait_bank(bank)
                pt = P[bank][:, 0:128]
                p.emit("tensor", lambda t, pt=pt, i=i, tc2=tc2:
                       t.transpose(pt, z2[:, i, tc2 * 128:(tc2 + 1) * 128], idf)
                       .then_inc(p.sems["s_mm"], 1))
                mmv = p.inc("s_mm", 1)
                p.wait("vector", "s_mm", mmv)
                p.emit("vector", lambda v, pt=pt, i=i: v.tensor_scalar(
                    otile[:, i * 128:(i + 1) * 128], pt,
                    f32a[:, 0:1], f32a[:, 1:2], ALU.mult, ALU.add)
                    .then_inc(p.sems["s_ve"], 1))
                vev = p.inc("s_ve", 1)
                bank_free[bank] = ("s_ve", vev)
            ot_ve = p.cur("s_ve")
            p.wait("sync", "s_ve", ot_ve)
            row = tt * 512 + tc2 * 128
            p.emit("sync", lambda s, row=row: s.dma_start(
                out=out[row:row + 128, :], in_=otile).then_inc(p.sems["s_out"], 16))
            p.inc("s_out", 16)
            p.wait("vector", "s_out", p.cur("s_out"))

    p.wait("sync", "s_out", p.cur("s_out"))

    with nc.Block() as block:
        @block.sync
        def _(sync):
            for fn in p.ops["sync"]:
                fn(sync)

        @block.gpsimd
        def _(g):
            for fn in p.ops["gpsimd"]:
                fn(g)

        @block.tensor
        def _(t):
            for fn in p.ops["tensor"]:
                fn(t)

        @block.vector
        def _(v):
            for fn in p.ops["vector"]:
                fn(v)

        @block.scalar
        def _(s):
            for fn in p.ops["scalar"]:
                fn(s)

    return nc


def _u8_low_byte(a_int32):
    v = a_int32.view(np.uint8).reshape(*a_int32.shape, 4)
    return np.ascontiguousarray(v[..., 0] if np.little_endian else v[..., 3])


_NC_CACHE = {}


def kernel(L0_attn_base, L0_attn_Wq, L0_attn_bq, L0_attn_Wk, L0_attn_bk,
           L0_attn_Wv, L0_attn_bv, L0_attn_Wo, L0_attn_bo,
           L0_ffn_wq_gate, L0_ffn_s_gate, L0_ffn_wq_up, L0_ffn_s_up,
           L0_ffn_wq_down, L0_ffn_s_down):
    bf16 = ml_dtypes.bfloat16
    base = np.asarray(L0_attn_base)
    assert base.dtype == bf16

    def pcol(b, nch):
        return np.ascontiguousarray(
            np.asarray(b, dtype=np.float32).reshape(nch, 128).T)

    shared = {
        "wqT": np.ascontiguousarray(np.asarray(L0_attn_Wq).T),
        "wkT": np.ascontiguousarray(np.asarray(L0_attn_Wk).T),
        "wvT": np.ascontiguousarray(np.asarray(L0_attn_Wv).T),
        "woT": np.ascontiguousarray(np.asarray(L0_attn_Wo).T),
        "bq": pcol(L0_attn_bq, DC),
        "bk": pcol(L0_attn_bk, DC),
        "bo": pcol(L0_attn_bo, DC),
        "bv": np.ascontiguousarray(np.asarray(L0_attn_bv)[None, :]),
        "wgT": np.ascontiguousarray(_u8_low_byte(np.asarray(L0_ffn_wq_gate))
                                    .reshape(22, 256, DC, 128)
                                    .transpose(0, 3, 2, 1).reshape(22, 128, DC * 256)),
        "wuT": np.ascontiguousarray(_u8_low_byte(np.asarray(L0_ffn_wq_up))
                                    .reshape(22, 256, DC, 128)
                                    .transpose(0, 3, 2, 1).reshape(22, 128, DC * 256)),
        "wdT": np.ascontiguousarray(_u8_low_byte(np.asarray(L0_ffn_wq_down))
                                    .reshape(16, 128, HC, 128)
                                    .transpose(0, 3, 2, 1).reshape(16, 128, HC * 128)),
        "sg": pcol(L0_ffn_s_gate, HC),
        "su": pcol(L0_ffn_s_up, HC),
        "sd": pcol(L0_ffn_s_down, DC),
        "idf": np.eye(128, dtype=np.float32),
    }

    in_maps = []
    for c in range(8):
        b, q = c // 4, c % 4
        m = dict(shared)
        m["x0fm"] = np.ascontiguousarray(base[b, q * T:(q + 1) * T, :].T)
        in_maps.append(m)

    nc = build_nc()
    res = run_bass_kernel_spmd(nc, in_maps, core_ids=list(range(8)))

    outv = np.empty((B, S, D), dtype=np.float32)
    for c in range(8):
        b, q = c // 4, c % 4
        outv[b, q * T:(q + 1) * T, :] = np.asarray(res.results[c]["out"])
    return outv
